# version 01
# baseline (speedup 1.0000x reference)
import contextlib
import numpy as np
import ml_dtypes
import concourse.bass as bass
import concourse.mybir as mybir
from concourse.bass_utils import run_bass_kernel_spmd

F32 = mybir.dt.float32
BF16 = mybir.dt.bfloat16
I32 = mybir.dt.int32
AF = mybir.ActivationFunctionType
ALU = mybir.AluOpType
AX = mybir.AxisListType

ENGINES = ("tensor", "vector", "scalar", "gpsimd", "sync")
T = 1024
NCH = 2
D = 4096
VT = 32
EPS = 1e-5


class Ctx:

    def __init__(self, nc):
        self.nc = nc
        self.st = contextlib.ExitStack()
        self.stacks = [self.st]
        self.ops = {e: [] for e in ENGINES}
        self.semh = {}
        self.cnt = {}
        self.consts = {}
        self.uid = 0
        self.fence = []
        self.cpool = self.sb("cpool", 128, 64, F32)
        self.scratch = self.sb("settle", 128, 12, F32)

    def name(self, p):
        self.uid += 1
        return f"{p}_{self.uid}"

    def sb(self, name, p, f, dt=F32):
        return self.stacks[-1].enter_context(
            self.nc.sbuf_tensor(self.name(name), [p, f], dt))

    def ps(self, name, p, f, dt=F32):
        return self.stacks[-1].enter_context(
            self.nc.psum_tensor(self.name(name), [p, f], dt))

    @contextlib.contextmanager
    def scope(self):
        s = contextlib.ExitStack()
        self.stacks.append(s)
        try:
            yield
        finally:
            self.fence = self.barrier()
            self.stacks.pop()
            s.close()

    def sem(self, name):
        if name not in self.semh:
            self.semh[name] = self.st.enter_context(self.nc.semaphore(name))
            self.cnt[name] = 0
        return self.semh[name]

    def tok(self, name):
        self.sem(name)
        return (name, self.cnt[name])

    CHAIN = {"vector": "vc", "scalar": "ac", "gpsimd": "gc"}

    def on(self, eng, build=None, waits=(), inc=None, dma=False):
        own = self.CHAIN.get(eng)
        nw = []
        for w in waits:
            if w is not None:
                sn, sv = w
                if sn in ("vc", "ac", "gc") and sn != own and sv > 0:
                    if self.cnt[sn] == sv:
                        self._emit_settle(sn)
                    sv += 1
                nw.append((sn, sv))
        waits = nw
        if eng in self.CHAIN and build is not None and not dma:
            cname = self.CHAIN[eng]
            self.sem(cname)
            assert inc is None or inc[0] == cname, (eng, inc)
            waits = list(waits) + [(cname, self.cnt[cname])]
            inc = (cname, 1)
        tokn = None
        semh = None
        amt = 0
        if inc is not None:
            iname, amt = inc
            self.sem(iname)
            self.cnt[iname] += amt
            tokn = (iname, self.cnt[iname])
            semh = self.semh[iname]
        wl = []
        for w in waits:
            if w is None:
                continue
            wn, wv = w
            if wv > 0:
                wl.append((self.semh[wn], wv))

        def replay(e, wl=wl, build=build, semh=semh, amt=amt):
            for h, v in wl:
                e.wait_ge(h, v)
            ins = build(e) if build is not None else None
            if semh is not None:
                assert ins is not None
                ins.then_inc(semh, amt)
        self.ops[eng].append(replay)
        return tokn

    def _emit_settle(self, cname):
        eng = {v: k for k, v in self.CHAIN.items()}[cname]
        scr = self.scratch
        if eng == "vector":
            self.on(eng, lambda e: e.tensor_scalar_add(
                out=scr[0:1, 0:4], in0=scr[0:1, 0:4], scalar1=0.0))
        elif eng == "scalar":
            self.on(eng, lambda e: e.activation(
                out=scr[0:1, 4:8], in_=scr[0:1, 4:8], func=AF.Identity))
        else:
            self.on(eng, lambda e: e.memset(scr[0:1, 8:12], 0.0))

    def dma(self, q, out, in_, waits=(), sem=None, **kw):
        inc = (sem or ("d_" + q), 16)
        return self.on(q, lambda e: e.dma_start(out=out, in_=in_, **kw),
                       waits=list(waits) + self.fence, inc=inc, dma=True)

    def barrier(self, engs=ENGINES):
        toks = []
        for e in engs:
            if e in self.CHAIN:
                toks.append(self.on(e, lambda ee: ee.nop()))
            else:
                toks.append(self.on(e, lambda ee: ee.nop(), inc=("bar_" + e, 1)))
        for e in engs:
            self.on(e, None, waits=toks)
        return toks

    def constf(self, val, rows=128):
        val = float(val)
        if val not in self.consts:
            self.consts[val] = len(self.consts)
            assert len(self.consts) <= 64
        return self.cpool[0:rows, self.consts[val]:self.consts[val] + 1]

    def finish(self):
        cpool = self.cpool
        sem_p = self.sem("prolog")
        nconst = [None]

        def pro(v):
            nconst[0] = len(self.consts)
            for val, col in self.consts.items():
                v.memset(cpool[:, col:col + 1], val)
            v.nop().then_inc(sem_p, 1)
        self.ops["vector"].insert(0, pro)
        self._nconst = nconst

        def wait_p(e):
            e.wait_ge(sem_p, 1)
        for e in ENGINES:
            if e != "vector":
                self.ops[e].insert(0, wait_p)

        blk = self.st.enter_context(self.nc.Block())
        for e in ENGINES:
            ops = self.ops[e]

            def body(engobj, ops=ops):
                for f in ops:
                    f(engobj)
            getattr(blk, e)(body)
        assert self._nconst[0] == len(self.consts), (
            "constf() called during replay (inside an op lambda) -- its "
            "const-pool cell was never initialised")


def bfnp(x):
    return np.ascontiguousarray(x).astype(ml_dtypes.bfloat16)


def tile_img(w, K, R):
    return np.ascontiguousarray(
        np.asarray(w).reshape(K, 128, R).transpose(1, 0, 2).reshape(128, K * R))


def tile_img_s(w, OM, K, R):
    w = np.asarray(w).reshape(K, 128, OM, R)
    return np.ascontiguousarray(
        w.transpose(2, 1, 0, 3).reshape(OM, 128, K * R))


def emit_ln_outer(ctx, x_dram, xln, ones_bf, onetok):
    with ctx.scope():
        ps_st = ctx.ps("lnps", 128, 1024)
        ps_sq = ctx.ps("lnpsq", 128, 1024)
        ps_bc = ctx.ps("lnbc", 128, 512)
        x2 = ctx.sb("lnx2", 128, 4 * T, BF16)
        rows = ctx.sb("lnrow", 1, 6 * T, F32)
        rbf = ctx.sb("lnrbf", 1, 2 * T, BF16)
        bcb = ctx.sb("lnbcb", 128, 2 * T, BF16)

        lnsem = ctx.name("lnld")
        for v in range(VT):
            ldall = ctx.dma("gpsimd", xln[:, v * T:(v + 1) * T],
                            x_dram[v * 128:(v + 1) * 128, :], sem=lnsem)

        sqtok = [None] * VT
        x2mm = [None] * VT
        for v in range(VT):
            w = [ldall]
            if v >= 4:
                w.append(x2mm[v - 4])
            sqtok[v] = ctx.on("scalar",
                              lambda e, v=v: e.activation(
                                  out=x2[:, (v % 4) * T:(v % 4 + 1) * T],
                                  in_=xln[:, v * T:(v + 1) * T], func=AF.Square),
                              waits=w)
        mtok = None
        for v in range(VT):
            for c in range(NCH):
                ctx.on("tensor",
                       lambda e, v=v, c=c: e.matmul(
                           ps_st[0:1, c * 512:(c + 1) * 512], ones_bf[:, 0:1],
                           xln[:, v * T + c * 512: v * T + (c + 1) * 512],
                           start=(v == 0), stop=(v == VT - 1)),
                       waits=[ldall, onetok])
                mtok = ctx.on("tensor",
                              lambda e, v=v, c=c: e.matmul(
                                  ps_sq[0:1, c * 512:(c + 1) * 512],
                                  ones_bf[:, 0:1],
                                  x2[:, (v % 4) * T + c * 512:(v % 4) * T + (c + 1) * 512],
                                  start=(v == 0), stop=(v == VT - 1)),
                              waits=[sqtok[v]],
                              inc=("pc", 1))
            x2mm[v] = mtok

        ctx.on("vector", lambda e: e.tensor_scalar(
            out=rows[:, 0:T], in0=ps_st[0:1, 0:T], scalar1=1.0 / D, scalar2=None,
            op0=ALU.mult), waits=[mtok])
        ctx.on("vector", lambda e: e.tensor_scalar(
            out=rows[:, T:2 * T], in0=ps_sq[0:1, 0:T], scalar1=1.0 / D,
            scalar2=None, op0=ALU.mult))
        ctx.on("vector", lambda e: e.tensor_mul(
            out=rows[:, 2 * T:3 * T], in0=rows[:, 0:T], in1=rows[:, 0:T]))
        ctx.on("vector", lambda e: e.tensor_sub(
            out=rows[:, 2 * T:3 * T], in0=rows[:, T:2 * T], in1=rows[:, 2 * T:3 * T]))
        vtok = ctx.on("vector", lambda e: e.tensor_scalar(
            out=rows[:, 2 * T:3 * T], in0=rows[:, 2 * T:3 * T], scalar1=EPS,
            scalar2=None, op0=ALU.add))
        atok = ctx.on("scalar", lambda e: e.activation(
            out=rows[:, 3 * T:4 * T], in_=rows[:, 2 * T:3 * T], func=AF.Sqrt),
            waits=[vtok])
        ctx.on("vector", lambda e: e.reciprocal(
            out=rows[:, 4 * T:5 * T], in_=rows[:, 3 * T:4 * T]), waits=[atok])
        ctx.on("vector", lambda e: e.tensor_mul(
            out=rows[:, 5 * T:6 * T], in0=rows[:, 0:T], in1=rows[:, 4 * T:5 * T]))
        rtok = ctx.on("vector", lambda e: e.tensor_copy(
            out=rbf[:, :], in_=rows[:, 4 * T:6 * T]), inc=("vc", 1))

        btok = None
        for half in range(2):
            for c in range(NCH):
                tk = ctx.on("tensor", lambda e, half=half, c=c: e.matmul(
                    ps_bc[:, :], ones_bf[0:1, 4:132],
                    rbf[:, half * T + c * 512: half * T + (c + 1) * 512],
                    start=True, stop=True),
                    waits=[rtok] + ([btok] if btok else []), inc=("pc", 1))
                btok = ctx.on("scalar", lambda e, half=half, c=c: e.activation(
                    out=bcb[:, half * T + c * 512: half * T + (c + 1) * 512],
                    in_=ps_bc[:, :], func=AF.Identity), waits=[tk], inc=("ac", 1))

        if getattr(ctx, "_rows_dump", None) is not None:
            ctx.on("vector", lambda e: e.tensor_copy(
                out=ctx._rows_dump[:, :], in_=rows[:, :]))
        if getattr(ctx, "_bcb_dump", None) is not None:
            ctx.on("vector", lambda e: e.tensor_copy(
                out=ctx._bcb_dump[:, :], in_=bcb[:, :]), waits=[btok])
        last = None
        for v in range(VT):
            w = [btok]
            if v == 0:
                w.append(sqtok[VT - 1])
            ctx.on("vector", lambda e, v=v: e.tensor_mul(
                out=xln[:, v * T:(v + 1) * T], in0=xln[:, v * T:(v + 1) * T],
                in1=bcb[:, 0:T]), waits=w)
            last = ctx.on("vector", lambda e, v=v: e.tensor_sub(
                out=xln[:, v * T:(v + 1) * T], in0=xln[:, v * T:(v + 1) * T],
                in1=bcb[:, T:2 * T]), inc=("vc", 1))
    return last


def emit_ones(ctx):
    ones = ctx.sb("ones", 128, 132, BF16)
    tok = ctx.on("vector", lambda e: e.memset(ones[:, :], 1.0), inc=("vc", 1))
    return ones, tok


def emit_l0f(ctx, io, dbg=False, stages=("rest", "main")):
    nc = ctx.nc
    xin, bwT, bb, AT, BT, rwT, ident, ehp_d, rpp_d, xout = (
        io["xin"], io["bwT"], io["bb"], io["AT"], io["BT"], io["rwT"],
        io["ident"], io["ehp"], io["rpp"], io["xout"])

    ones, onetok = emit_ones(ctx)
    xln = ctx.sb("xln", 128, VT * T, BF16)
    if dbg:
        ctx._rows_dump = ctx.sb("rowdump", 1, 6 * T, F32)
        ctx._bcb_dump = ctx.sb("bcbdump", 128, 2 * T, BF16)
    lnall = emit_ln_outer(ctx, xin, xln, ones, onetok)

    res_sb = ctx.sb("res_sb", 64, T, BF16)
    cwT = ctx.sb("cwT", 16, T, BF16)
    ctx._dbg_xln, ctx._dbg_res, ctx._dbg_cw = xln, res_sb, cwT
    lgall = ctx.sb("lgall", 128, 8 * 16, F32)
    pall = ctx.sb("pall", 128, 8 * 16, F32)
    ctx._dbg_lg, ctx._dbg_p = lgall, pall
    Csb = ctx.sb("Csb", 128, 8 * T, BF16)

    if "rest" not in stages:
        ctx.on("vector", lambda e: e.memset(res_sb[:, :], 0.0), inc=("vc", 1))
        ctx.on("vector", lambda e: e.memset(cwT[:, :], 0.0), inc=("vc", 1))
        ctx.on("vector", lambda e: e.memset(Csb[:, :], 0.0), inc=("vc", 1))
        ctx.on("vector", lambda e: e.memset(lgall[:, :], 0.0))
        ctx.on("vector", lambda e: e.memset(pall[:, :], 0.0), inc=("vc", 1))
        call = ctx.tok("vc")
        if "main" in stages:
            emit_l0f_main(ctx, io, xln, lnall, Csb, call)
        if dbg:
            emit_l0f_dbg(ctx, io)
        return
    with ctx.scope():
        atb = ctx.sb("atb", 128, 32 * 64, BF16)
        csem = ctx.name("l0fc")
        attok = ctx.dma("gpsimd", atb[:, :], AT[:, :], sem=csem)
        psr = ctx.ps("psr", 128, 1024)
        rtok = None
        c3 = (csem, 48)
        for c in range(NCH):
            for k in range(VT):
                rtok = ctx.on("tensor", lambda e, c=c, k=k: e.matmul(
                    psr[0:64, c * 512:(c + 1) * 512], atb[:, k * 64:(k + 1) * 64],
                    xln[:, k * T + c * 512:k * T + (c + 1) * 512],
                    start=(k == 0), stop=(k == VT - 1)),
                    waits=[c3, lnall] if k == 0 else [],
                    inc=("pc", 1) if k == VT - 1 else None)
        restok = ctx.on("vector", lambda e: e.tensor_copy(
            out=res_sb[:, :], in_=psr[0:64, 0:T]), waits=[rtok], inc=("vc", 1))

        rwb = ctx.sb("rwb", 128, 32 * 16, BF16)
        rwtok = ctx.dma("gpsimd", rwb[:, :], rwT[:, :], sem=csem)
        idb = ctx.sb("idb", 128, 128, BF16)
        idtok = ctx.dma("gpsimd", idb[:, :], ident[:, :], sem=csem)
        psl = ctx.ps("psl", 128, 16)
        pst = ctx.ps("pst", 128, 128, BF16)
        lg = ctx.sb("lg", 128, 16, F32)
        sc1 = ctx.sb("sc1", 128, 8, F32)
        P = ctx.sb("Psm", 128, 16, F32)
        msk = ctx.sb("msk", 128, 3 * 16, F32)
        cw = ctx.sb("cw", 128, 16, BF16)

        cwlast = None
        for tm in range(8):
            mmt = None
            for k in range(VT):
                mmt = ctx.on("tensor", lambda e, tm=tm, k=k: e.matmul(
                    psl[:, :], xln[:, k * T + tm * 128:k * T + (tm + 1) * 128],
                    rwb[:, k * 16:(k + 1) * 16],
                    start=(k == 0), stop=(k == VT - 1)),
                    waits=([c3, lnall, ctx.tok("ac")] if k == 0 else []),
                    inc=("pc", 1) if k == VT - 1 else None)
            ev = ctx.on("scalar", lambda e: e.activation(
                out=lg[:, :], in_=psl[:, :], func=AF.Identity),
                waits=[mmt], inc=("ac", 1))
            ctx.on("vector", lambda e, tm=tm: e.tensor_copy(
                out=lgall[:, tm * 16:(tm + 1) * 16], in_=lg[:, :]), waits=[ev])
            ctx.on("vector", lambda e: e.reduce_max(
                out=sc1[:, 0:1], in_=lg[:, :], axis=AX.X), waits=[ev])
            ctx.on("vector", lambda e: e.tensor_scalar(
                out=sc1[:, 1:2], in0=sc1[:, 0:1], scalar1=-1.0, scalar2=None,
                op0=ALU.mult), inc=("vc", 1))
            nm = ctx.tok("vc")
            ex = ctx.on("scalar", lambda e: e.activation(
                out=P[:, :], in_=lg[:, :], func=AF.Exp, bias=sc1[:, 1:2],
                scale=1.0), waits=[nm], inc=("ac", 1))
            ctx.on("vector", lambda e: e.reduce_sum(
                out=sc1[:, 2:3], in_=P[:, :], axis=AX.X), waits=[ex])
            ctx.on("vector", lambda e: e.reciprocal(
                out=sc1[:, 3:4], in_=sc1[:, 2:3]))
            ctx.on("vector", lambda e: e.tensor_scalar(
                out=P[:, :], in0=P[:, :], scalar1=sc1[:, 3:4], scalar2=None,
                op0=ALU.mult))
            ctx.on("vector", lambda e, tm=tm: e.tensor_copy(
                out=pall[:, tm * 16:(tm + 1) * 16], in_=P[:, :]))
            ctx.on("vector", lambda e: e.reduce_max(
                out=sc1[:, 4:5], in_=P[:, :], axis=AX.X))
            ctx.on("vector", lambda e: e.tensor_scalar(
                out=msk[:, 0:16], in0=P[:, :], scalar1=sc1[:, 4:5], scalar2=None,
                op0=ALU.is_equal))
            ctx.on("vector", lambda e: e.scalar_tensor_tensor(
                out=msk[:, 16:32], in0=msk[:, 0:16], scalar=-2.0, in1=P[:, :],
                op0=ALU.mult, op1=ALU.add))
            ctx.on("vector", lambda e: e.reduce_max(
                out=sc1[:, 5:6], in_=msk[:, 16:32], axis=AX.X))
            ctx.on("vector", lambda e: e.tensor_scalar(
                out=msk[:, 32:48], in0=P[:, :], scalar1=sc1[:, 5:6], scalar2=None,
                op0=ALU.is_equal))
            ctx.on("vector", lambda e: e.tensor_add(
                out=msk[:, 0:16], in0=msk[:, 0:16], in1=msk[:, 32:48]))
            ctx.on("vector", lambda e: e.tensor_mul(
                out=msk[:, 0:16], in0=msk[:, 0:16], in1=P[:, :]))
            ctx.on("vector", lambda e: e.tensor_add(
                out=sc1[:, 6:7], in0=sc1[:, 4:5], in1=sc1[:, 5:6]))
            ctx.on("vector", lambda e: e.tensor_scalar(
                out=sc1[:, 6:7], in0=sc1[:, 6:7], scalar1=1e-6, scalar2=None,
                op0=ALU.add))
            ctx.on("vector", lambda e: e.reciprocal(
                out=sc1[:, 7:8], in_=sc1[:, 6:7]))
            cwt = ctx.on("vector", lambda e: e.tensor_scalar(
                out=cw[:, :], in0=msk[:, 0:16], scalar1=sc1[:, 7:8], scalar2=None,
                op0=ALU.mult), inc=("vc", 1))
            w = [cwt, c3]
            if cwlast is not None:
                w.append(cwlast)
            tt = ctx.on("tensor", lambda e: e.transpose(
                pst[0:16, :], cw[:, :], idb[:, :]), waits=w, inc=("pc", 1))
            cwlast = ctx.on("vector", lambda e, tm=tm: e.tensor_copy(
                out=cwT[:, tm * 128:(tm + 1) * 128], in_=pst[0:16, :]),
                waits=[tt], inc=("vc", 1))

        ehp = ctx.sb("ehp", 16, 8 * 128, BF16)
        rpp = ctx.sb("rpp", 64, 128, BF16)
        etok = ctx.dma("gpsimd", ehp[:, :], ehp_d[:, :], sem=csem)
        ptok = ctx.dma("gpsimd", rpp[:, :], rpp_d[:, :], sem=csem)
        ps1 = ctx.ps("cps1", 128, 512)
        ps2 = ctx.ps("cps2", 128, 512)
        rr = ctx.sb("rr", 128, 512, BF16)
        prev = None
        for em in range(8):
            for c in range(NCH):
                w1 = [cwlast, (csem, 80), prev]
                w2 = [restok, (csem, 80), prev]
                t1 = ctx.on("tensor", lambda e, em=em, c=c: e.matmul(
                    ps1[:, :], ehp[:, em * 128:(em + 1) * 128],
                    cwT[:, c * 512:(c + 1) * 512], start=True, stop=True),
                    waits=w1, inc=("pc", 1))
                t2 = ctx.on("tensor", lambda e, c=c: e.matmul(
                    ps2[:, :], rpp[:, :], res_sb[:, c * 512:(c + 1) * 512],
                    start=True, stop=True), waits=w2, inc=("pc", 1))
                t3 = ctx.on("scalar", lambda e: e.activation(
                    out=rr[:, :], in_=ps2[:, :], func=AF.Identity),
                    waits=[t2, prev], inc=("ac", 1))
                prev = ctx.on("vector", lambda e, em=em, c=c: e.scalar_tensor_tensor(
                    out=Csb[:, em * T + c * 512:em * T + (c + 1) * 512],
                    in0=ps1[:, :], scalar=0.5, in1=rr[:, :],
                    op0=ALU.mult, op1=ALU.mult), waits=[t1, t3], inc=("vc", 1))
        call = prev

    if "main" in stages:
        emit_l0f_main(ctx, io, xln, lnall, Csb, call)
    if dbg:
        emit_l0f_dbg(ctx, io)


def emit_l0f_main(ctx, io, xln, lnall, Csb, call):
    xin, bwT, bb, BT, xout = io["xin"], io["bwT"], io["bb"], io["BT"], io["xout"]
    with ctx.scope():
        bbs = ctx.sb("bbs", 128, 32, F32)
        bbsem = ctx.name("bbs")
        bbtok = ctx.dma("gpsimd", bbs[:, :], bb[:, :], sem=bbsem)
        NBW = 2
        bwb = [ctx.sb("bwb", 128, 32 * 128, BF16) for _ in range(NBW)]
        btb = [ctx.sb("btb", 128, 8 * 128, BF16) for _ in range(NBW)]
        pso = [ctx.ps("pso", 128, 512) for _ in range(2)]
        xo = [ctx.sb("xo", 128, 512, F32) for _ in range(2)]
        xn = [ctx.sb("xn", 128, 512, F32) for _ in range(2)]
        users = [None] * NBW
        xotok = [None] * 2
        evtok = [None] * 2
        sttok = [None] * 2
        laststore = None
        bwsem = [ctx.name("bw") for _ in range(NBW)]
        xosem = [ctx.name("xo") for _ in range(2)]
        stsem = [ctx.name("stx") for _ in range(2)]

        for om in range(32):
            sl = om % NBW
            ldw = [users[sl]] if users[sl] else []
            ctx.dma("gpsimd", bwb[sl][:, :], bwT[om], waits=ldw, sem=bwsem[sl])
            lt = ctx.dma("gpsimd", btb[sl][:, :], BT[om], sem=bwsem[sl])
            for c in range(NCH):
                pi = (om * NCH + c) % 2
                mmw = [lt, lnall, call, evtok[pi]]
                mmt = None
                for k in range(VT):
                    mmt = ctx.on("tensor", lambda e, sl=sl, c=c, k=k, pi=pi: e.matmul(
                        pso[pi][:, :], bwb[sl][:, k * 128:(k + 1) * 128],
                        xln[:, k * T + c * 512:k * T + (c + 1) * 512],
                        start=(k == 0), stop=False),
                        waits=mmw if k == 0 else [])
                for j in range(8):
                    mmt = ctx.on("tensor", lambda e, sl=sl, c=c, j=j, pi=pi: e.matmul(
                        pso[pi][:, :], btb[sl][:, j * 128:(j + 1) * 128],
                        Csb[:, j * T + c * 512:j * T + (c + 1) * 512],
                        start=False, stop=(j == 7)),
                        inc=("pc", 1) if j == 7 else None)
                users[sl] = mmt
                xotok[pi] = ctx.dma("scalar", xo[pi][:, :],
                                    xin[om * 128:(om + 1) * 128,
                                        c * 512:(c + 1) * 512],
                                    waits=[evtok[pi]] if evtok[pi] else [],
                                    sem=xosem[pi])
                evtok[pi] = ctx.on("vector",
                                   lambda e, om=om, pi=pi: e.scalar_tensor_tensor(
                                       out=xn[pi][:, :], in0=pso[pi][:, :],
                                       scalar=bbs[:, om:om + 1], in1=xo[pi][:, :],
                                       op0=ALU.add, op1=ALU.add),
                                   waits=[mmt, xotok[pi], bbtok, sttok[pi]],
                                   inc=("vc", 1))
                sttok[pi] = ctx.dma("sync",
                                    xout[om * 128:(om + 1) * 128,
                                         c * 512:(c + 1) * 512],
                                    xn[pi][:, :], waits=[evtok[pi]],
                                    sem=stsem[pi])
        ctx.on("sync", None, waits=[sttok[0], sttok[1]])


def emit_l0f_dbg(ctx, io):
    ctx.barrier()
    dx = ctx.dma("gpsimd", io["dbg_xln"].rearrange("v p t -> p v t"),
                 ctx._dbg_xln.ap().rearrange("p (v t) -> p v t", v=VT),
                 waits=[ctx.tok("vc")])
    dr = ctx.dma("gpsimd", io["dbg_res"], ctx._dbg_res[:, :])
    dc = ctx.dma("gpsimd", io["dbg_cw"], ctx._dbg_cw[:, :])
    dl = ctx.dma("gpsimd", io["dbg_lg"], ctx._dbg_lg[:, :])
    dp = ctx.dma("gpsimd", io["dbg_p"], ctx._dbg_p[:, :])
    dw = ctx.dma("gpsimd", io["dbg_rows"], ctx._rows_dump[:, :])
    db = ctx.dma("gpsimd", io["dbg_bcb"], ctx._bcb_dump[:, :])
    ctx.on("sync", None, waits=[dx, dr, dc, dl, dp, dw])


def build_l0f(dbg=False, stages=("rest", "main")):
    nc = bass.Bass("TRN2", target_bir_lowering=False)
    io = dict(
        xin=nc.dram_tensor("xin", [D, T], F32, kind="ExternalInput").ap(),
        bwT=nc.dram_tensor("bwT", [32, 128, D], F32, kind="ExternalInput").ap(),
        bb=nc.dram_tensor("bb", [128, 32], F32, kind="ExternalInput").ap(),
        AT=nc.dram_tensor("AT", [128, 32 * 64], F32, kind="ExternalInput").ap(),
        BT=nc.dram_tensor("BT", [32, 128, 8 * 128], F32, kind="ExternalInput").ap(),
        rwT=nc.dram_tensor("rwT", [128, 32 * 16], F32, kind="ExternalInput").ap(),
        ident=nc.dram_tensor("ident", [128, 128], F32, kind="ExternalInput").ap(),
        ehp=nc.dram_tensor("ehp", [16, 8 * 128], F32, kind="ExternalInput").ap(),
        rpp=nc.dram_tensor("rpp", [64, 128], F32, kind="ExternalInput").ap(),
        xout=nc.dram_tensor("xout", [D, T], F32, kind="ExternalOutput").ap(),
    )
    if dbg:
        io["dbg_xln"] = nc.dram_tensor("dbg_xln", [VT, 128, T], BF16,
                                       kind="ExternalOutput").ap()
        io["dbg_res"] = nc.dram_tensor("dbg_res", [64, T], BF16,
                                       kind="ExternalOutput").ap()
        io["dbg_cw"] = nc.dram_tensor("dbg_cw", [16, T], BF16,
                                      kind="ExternalOutput").ap()
        io["dbg_lg"] = nc.dram_tensor("dbg_lg", [128, 8 * 16], F32,
                                      kind="ExternalOutput").ap()
        io["dbg_p"] = nc.dram_tensor("dbg_p", [128, 8 * 16], F32,
                                     kind="ExternalOutput").ap()
        io["dbg_rows"] = nc.dram_tensor("dbg_rows", [1, 6 * T], F32,
                                        kind="ExternalOutput").ap()
        io["dbg_bcb"] = nc.dram_tensor("dbg_bcb", [128, 2 * T], BF16,
                                       kind="ExternalOutput").ap()
    ctx = Ctx(nc)
    emit_l0f(ctx, io, dbg=dbg, stages=stages)
    ctx.finish()
    return nc


def stage_l0f_common(inp):
    base_w = np.asarray(inp["L0_ffn_base_w"])
    ehp = np.zeros((16, 8 * 128), np.float32)
    for em in range(8):
        for j in range(128):
            ehp[(em * 128 + j) // 64, em * 128 + j] = 1.0
    rpp = np.zeros((64, 128), np.float32)
    for j in range(128):
        rpp[j % 64, j] = 1.0
    return dict(
        bwT=tile_img_s(base_w.T, 32, 32, 128),
        bb=np.ascontiguousarray(
            np.asarray(inp["L0_ffn_base_b"]).reshape(32, 128).T),
        AT=tile_img(np.asarray(inp["L0_ffn_A"]).T, 32, 64),
        BT=tile_img_s(
            np.asarray(inp["L0_ffn_B"]).transpose(0, 2, 1).reshape(16 * 64, D),
            32, 8, 128),
        rwT=tile_img(np.asarray(inp["L0_ffn_router_w"]).T, 32, 16),
        ident=np.eye(128, dtype=np.float32),
        ehp=ehp, rpp=rpp)


def run_l0f(xs_per_core, inp, dbg=False, stages=("rest", "main")):
    nc = build_l0f(dbg, stages)
    common = stage_l0f_common(inp)
    in_maps = [dict(common, xin=np.ascontiguousarray(x)) for x in xs_per_core]
    res = run_bass_kernel_spmd(nc, in_maps, core_ids=list(range(8)))
    if dbg:
        return res.results
    return [r["xout"] for r in res.results]


def kernel(**inp):
    raise NotImplementedError("under construction")


IB = 86


def emit_l1f(ctx, io):
    nc = ctx.nc
    xin, xout = io["xin"], io["xout"]
    gq, uq, sg, su = io["gq"], io["uq"], io["sg"], io["su"]
    dq, sd = io["dq"], io["sd"]
    hdram = io["hdram"]

    ones, onetok = emit_ones(ctx)
    nb8 = ctx.constf(-8.0)

    with ctx.scope():
        xln = ctx.sb("xln", 128, VT * T, BF16)
        lnall = emit_ln_outer(ctx, xin, xln, ones, onetok)

        NS = 2
        qtile = [ctx.sb("qtile", 128, 4096, I32) for _ in range(NS)]
        stile = [ctx.sb("stile", 128, 4096, BF16) for _ in range(NS)]
        qcast = ctx.sb("qcast", 128, 4096, BF16)
        wg = [ctx.sb("wg", 128, 4096, BF16) for _ in range(NS)]
        wu = [ctx.sb("wu", 128, 4096, BF16) for _ in range(NS)]
        psg = ctx.ps("psg", 128, 1024)
        psu = ctx.ps("psu", 128, 1024)
        t1 = ctx.sb("t1", 128, 1024, BF16)
        hb = [ctx.sb("hb", 128, 1024, BF16) for _ in range(NS)]
        qsem = [ctx.name("q") for _ in range(NS)]
        hssem = [ctx.name("hs") for _ in range(NS)]
        wtok = [None] * NS
        usetok = [None] * NS
        evtok = [None, None]
        hstok = [None] * NS

        for ib in range(IB):
            sl = ib % NS
            w = [usetok[sl]] if usetok[sl] else []
            ctx.dma("gpsimd", qtile[sl][:, :], gq[ib], waits=w, sem=qsem[sl])
            ctx.dma("gpsimd", stile[sl][:, :], sg[ib], sem=qsem[sl])
            nld = 16 * 4 * (ib // NS + 1)
            ct = ctx.on("scalar", lambda e, sl=sl: e.activation(
                out=qcast[:, :], in_=qtile[sl][:, :], func=AF.Identity,
                bias=nb8, scale=1.0),
                waits=[(qsem[sl], 16 * 4 * (ib // NS) + 32), ctx.tok("vc")])
            wtok[sl] = ctx.on("vector", lambda e, sl=sl: e.tensor_mul(
                out=wg[sl][:, :], in0=qcast[:, :], in1=stile[sl][:, :]),
                waits=[ct])
            ctx.dma("gpsimd", qtile[sl][:, :], uq[ib], waits=[wtok[sl]],
                    sem=qsem[sl])
            ctx.dma("gpsimd", stile[sl][:, :], su[ib], sem=qsem[sl])
            ct = ctx.on("scalar", lambda e, sl=sl: e.activation(
                out=qcast[:, :], in_=qtile[sl][:, :], func=AF.Identity,
                bias=nb8, scale=1.0), waits=[(qsem[sl], nld), ctx.tok("vc")])
            wtok[sl] = ctx.on("vector", lambda e, sl=sl: e.tensor_mul(
                out=wu[sl][:, :], in0=qcast[:, :], in1=stile[sl][:, :]),
                waits=[ct])
            mmu = None
            for tch in range(NCH):
                for g in range(VT):
                    ctx.on("tensor", lambda e, sl=sl, g=g, tch=tch: e.matmul(
                        psg[:, tch * 512:(tch + 1) * 512],
                        wg[sl][:, g * 128:(g + 1) * 128],
                        xln[:, g * T + tch * 512:g * T + (tch + 1) * 512],
                        start=(g == 0), stop=(g == VT - 1)),
                        waits=[wtok[sl], lnall, evtok[tch]] if g == 0 else [])
                for g in range(VT):
                    mmu = ctx.on("tensor", lambda e, sl=sl, g=g, tch=tch: e.matmul(
                        psu[:, tch * 512:(tch + 1) * 512],
                        wu[sl][:, g * 128:(g + 1) * 128],
                        xln[:, g * T + tch * 512:g * T + (tch + 1) * 512],
                        start=(g == 0), stop=(g == VT - 1)),
                        waits=[wtok[sl]] if g == 0 else [],
                        inc=("pc", 1) if g == VT - 1 else None)
            usetok[sl] = mmu
            for tch in range(NCH):
                st_ = ctx.on("scalar", lambda e, tch=tch: e.activation(
                    out=t1[:, tch * 512:(tch + 1) * 512],
                    in_=psg[:, tch * 512:(tch + 1) * 512], func=AF.Silu),
                    waits=[usetok[sl], hstok[sl], evtok[tch]])
                evtok[tch] = ctx.on("vector", lambda e, sl=sl, tch=tch: e.tensor_mul(
                    out=hb[sl][:, tch * 512:(tch + 1) * 512],
                    in0=t1[:, tch * 512:(tch + 1) * 512],
                    in1=psu[:, tch * 512:(tch + 1) * 512]), waits=[st_])
            hstok[sl] = ctx.dma("sync", hdram[ib], hb[sl][:, :],
                                waits=[evtok[0], evtok[1]], sem=hssem[sl])
        ctx.on("sync", None, waits=[t for t in hstok if t])

    with ctx.scope():
        hsb = ctx.sb("hsb", 128, IB * T, BF16)
        hsem = ctx.name("hld")
        for ib in range(IB):
            hldall = ctx.dma("sync", hsb[:, ib * T:(ib + 1) * T], hdram[ib],
                             sem=hsem)
        NS = 2
        qd = [ctx.sb("qd", 128, 512, I32) for _ in range(NS)]
        sdt = [ctx.sb("sdt", 128, 512, BF16) for _ in range(NS)]
        qdc = ctx.sb("qdc", 128, 512, BF16)
        wd = [ctx.sb("wd", 128, 512, BF16) for _ in range(NS)]
        psd = [ctx.ps("psd", 128, 1024) for _ in range(4)]
        xo = [ctx.sb("xo", 128, 512, F32) for _ in range(2)]
        xn = [ctx.sb("xn", 128, 512, F32) for _ in range(2)]
        dsem = [ctx.name("dq") for _ in range(NS)]
        xosem = [ctx.name("xod") for _ in range(2)]
        stsem = [ctx.name("std") for _ in range(2)]
        wtok = [None] * NS
        usetok = [None] * NS
        xotok = [None, None]
        evtok = [None, None]
        sttok = [None, None]
        ogdone = None

        for og in range(8):
            mmt = None
            for it in range(IB):
                k = og * IB + it
                sl = k % NS
                w = [usetok[sl]] if usetok[sl] else []
                ctx.dma("gpsimd", qd[sl][:, :],
                        dq[it * 128:(it + 1) * 128, og * 512:(og + 1) * 512],
                        waits=w, sem=dsem[sl])
                ctx.dma("gpsimd", sdt[sl][:, :],
                        sd[:, it, og * 512:(og + 1) * 512], sem=dsem[sl])
                ct = ctx.on("scalar", lambda e, sl=sl: e.activation(
                    out=qdc[:, :], in_=qd[sl][:, :], func=AF.Identity,
                    bias=nb8, scale=1.0),
                    waits=[(dsem[sl], 32 * (k // NS + 1)), ctx.tok("vc")])
                wtok[sl] = ctx.on("vector", lambda e, sl=sl: e.tensor_mul(
                    out=wd[sl][:, :], in0=qdc[:, :], in1=sdt[sl][:, :]),
                    waits=[ct])
                mmw = [wtok[sl], hldall]
                if it == 0:
                    mmw.append(ogdone)
                for om4 in range(4):
                    for tch in range(NCH):
                        mmt = ctx.on("tensor",
                                     lambda e, sl=sl, it=it, om4=om4, tch=tch: e.matmul(
                                         psd[om4][:, tch * 512:(tch + 1) * 512],
                                         wd[sl][:, om4 * 128:(om4 + 1) * 128],
                                         hsb[:, it * T + tch * 512:it * T + (tch + 1) * 512],
                                         start=(it == 0), stop=(it == IB - 1)),
                                     waits=mmw if om4 == 0 and tch == 0 else [],
                                     inc=("pc", 1) if om4 == 3 and tch == NCH - 1
                                     else None)
                usetok[sl] = mmt
            for om4 in range(4):
                om = og * 4 + om4
                for tch in range(NCH):
                    pi = (om4 * NCH + tch) % 2
                    xotok[pi] = ctx.dma("scalar", xo[pi][:, :],
                                        xin[om * 128:(om + 1) * 128,
                                            tch * 512:(tch + 1) * 512],
                                        waits=[evtok[pi]] if evtok[pi] else [],
                                        sem=xosem[pi])
                    evtok[pi] = ctx.on("vector",
                                       lambda e, om4=om4, tch=tch, pi=pi: e.tensor_add(
                                           out=xn[pi][:, :],
                                           in0=psd[om4][:, tch * 512:(tch + 1) * 512],
                                           in1=xo[pi][:, :]),
                                       waits=[mmt, xotok[pi], sttok[pi]])
                    sttok[pi] = ctx.dma("sync",
                                        xout[om * 128:(om + 1) * 128,
                                             tch * 512:(tch + 1) * 512],
                                        xn[pi][:, :], waits=[evtok[pi]],
                                        sem=stsem[pi])
            ogdone = evtok[1] or evtok[0]
        ctx.on("sync", None, waits=[sttok[0], sttok[1]])


def build_l1f(dbg=False):
    nc = bass.Bass("TRN2", target_bir_lowering=False)
    io = dict(
        xin=nc.dram_tensor("xin", [D, T], F32, kind="ExternalInput").ap(),
        gq=nc.dram_tensor("gq", [IB, 128, 4096], I32, kind="ExternalInput").ap(),
        uq=nc.dram_tensor("uq", [IB, 128, 4096], I32, kind="ExternalInput").ap(),
        sg=nc.dram_tensor("sg", [IB, 128, 4096], F32, kind="ExternalInput").ap(),
        su=nc.dram_tensor("su", [IB, 128, 4096], F32, kind="ExternalInput").ap(),
        dq=nc.dram_tensor("dq", [11008, 4096], I32, kind="ExternalInput").ap(),
        sd=nc.dram_tensor("sd", [128, IB, 4096], F32, kind="ExternalInput").ap(),
        hdram=nc.dram_tensor("hdram", [IB, 128, T], BF16,
                             kind="ExternalOutput" if dbg else "Internal").ap(),
        xout=nc.dram_tensor("xout", [D, T], F32, kind="ExternalOutput").ap(),
    )
    ctx = Ctx(nc)
    emit_l1f(ctx, io)
    ctx.finish()
    return nc


def stage_l1f_common(inp):
    gate_q = np.asarray(inp["L1_ffn_gate_q"])
    up_q = np.asarray(inp["L1_ffn_up_q"])
    down_q = np.asarray(inp["L1_ffn_down_q"])
    gate_s = np.asarray(inp["L1_ffn_gate_s"])
    up_s = np.asarray(inp["L1_ffn_up_s"])
    down_s = np.asarray(inp["L1_ffn_down_s"])

    def col_img(q):
        q = q.reshape(32, 128, IB, 128)
        return np.ascontiguousarray(q.transpose(2, 1, 0, 3).reshape(IB, 128, 4096))

    def s_img(s):
        s = s.reshape(32, IB, 128).transpose(1, 0, 2)
        s = np.broadcast_to(s[:, None, :, :], (IB, 128, 32, 128))
        return np.ascontiguousarray(s.reshape(IB, 128, 4096))

    return dict(
        gq=col_img(gate_q), uq=col_img(up_q),
        sg=s_img(gate_s), su=s_img(up_s),
        dq=np.ascontiguousarray(down_q),
        sd=np.ascontiguousarray(
            np.broadcast_to(down_s[None, :, :], (128, IB, 4096))),
    )


def run_l1f(xs_per_core, inp):
    nc = build_l1f()
    common = stage_l1f_common(inp)
    in_maps = [dict(common, xin=np.ascontiguousarray(x)) for x in xs_per_core]
    res = run_bass_kernel_spmd(nc, in_maps, core_ids=list(range(8)))
    return [r["xout"] for r in res.results]


def emit_l1a_qkv(ctx, io):
    xin, wqk, wv, qf, kf, vt = (io["xin"], io["wqk"], io["wv"], io["qf"],
                                io["kf"], io["vt"])
    ones, onetok = emit_ones(ctx)
    xln = ctx.sb("xln", 128, VT * T, BF16)
    lnall = emit_ln_outer(ctx, xin, xln, ones, onetok)

    with ctx.scope():
        NS = 2
        wb = [ctx.sb("wb", 128, 32 * 128, BF16) for _ in range(NS)]
        ps = [ctx.ps("psqk", 128, 1024) for _ in range(2)]
        ob = [ctx.sb("ob", 128, 1024, BF16) for _ in range(2)]
        wsem = [ctx.name("wq") for _ in range(NS)]
        stsem = ctx.name("stq")
        usetok = [None] * NS
        evtok = [None, None]
        sttok = None
        for om in range(40):
            sl = om % NS
            w = [usetok[sl]] if usetok[sl] else []
            lt = ctx.dma("gpsimd", wb[sl][:, :], wqk[om], waits=w, sem=wsem[sl])
            mmt = None
            pi = om % 2
            for tch in range(NCH):
                for k in range(VT):
                    mmt = ctx.on("tensor", lambda e, sl=sl, k=k, tch=tch, pi=pi: e.matmul(
                        ps[pi][:, tch * 512:(tch + 1) * 512],
                        wb[sl][:, k * 128:(k + 1) * 128],
                        xln[:, k * T + tch * 512:k * T + (tch + 1) * 512],
                        start=(k == 0), stop=(k == VT - 1)),
                        waits=[(wsem[sl], 16 * (om // NS + 1)), lnall, evtok[pi]]
                        if k == 0 and tch == 0 else [],
                        inc=("pc", 1) if k == VT - 1 and tch == NCH - 1 else None)
            usetok[sl] = mmt
            evtok[pi] = ctx.on("vector", lambda e, pi=pi: e.tensor_copy(
                out=ob[pi][:, :], in_=ps[pi][:, 0:T]),
                waits=[mmt, sttok])
            dst = (qf[om * 128:(om + 1) * 128, :] if om < 32 else
                   kf[(om - 32) * 128:(om - 31) * 128, :])
            sttok = ctx.dma("sync", dst, ob[pi][:, :], waits=[evtok[pi]],
                            sem=stsem)
        qkdone = sttok

    with ctx.scope():
        wvb = ctx.sb("wvb", 128, 32 * 1024, BF16)
        vsem = ctx.name("wv")
        wvtok = ctx.dma("gpsimd", wvb[:, :], wv[:, :], sem=vsem)
        psv = [ctx.ps("psv", 128, 512) for _ in range(2)]
        ovb = [ctx.sb("ovb", 128, 512, BF16) for _ in range(2)]
        stsem2 = ctx.name("stv")
        evtok = [None, None]
        sttok = [None, None]
        for tm in range(8):
            for vch in range(2):
                pi = (tm * 2 + vch) % 2
                mmt = None
                for k in range(VT):
                    mmt = ctx.on("tensor", lambda e, k=k, tm=tm, vch=vch, pi=pi: e.matmul(
                        psv[pi][:, :],
                        xln[:, k * T + tm * 128:k * T + (tm + 1) * 128],
                        wvb[:, k * 1024 + vch * 512:k * 1024 + (vch + 1) * 512],
                        start=(k == 0), stop=(k == VT - 1)),
                        waits=[wvtok, lnall, evtok[pi]] if k == 0 else [],
                        inc=("pc", 1) if k == VT - 1 else None)
                evtok[pi] = ctx.on("vector", lambda e, pi=pi: e.tensor_copy(
                    out=ovb[pi][:, :], in_=psv[pi][:, :]),
                    waits=[mmt, sttok[pi]])
                sttok[pi] = ctx.dma("sync",
                                    vt[tm * 128:(tm + 1) * 128,
                                       vch * 512:(vch + 1) * 512],
                                    ovb[pi][:, :], waits=[evtok[pi]], sem=stsem2)
        ctx.on("sync", None, waits=[sttok[0], sttok[1], qkdone])


def build_l1a_qkv():
    nc = bass.Bass("TRN2", target_bir_lowering=False)
    io = dict(
        xin=nc.dram_tensor("xin", [D, T], F32, kind="ExternalInput").ap(),
        wqk=nc.dram_tensor("wqk", [40, 128, 32 * 128], F32, kind="ExternalInput").ap(),
        wv=nc.dram_tensor("wv", [128, 32 * 1024], F32, kind="ExternalInput").ap(),
        qf=nc.dram_tensor("qf", [D, T], BF16, kind="ExternalOutput").ap(),
        kf=nc.dram_tensor("kf", [1024, T], BF16, kind="ExternalOutput").ap(),
        vt=nc.dram_tensor("vt", [T, 1024], BF16, kind="ExternalOutput").ap(),
    )
    ctx = Ctx(nc)
    emit_l1a_qkv(ctx, io)
    ctx.finish()
    return nc


SCL = 0.08838834764831845


def emit_l1a_att_m(ctx, io, moff, dbg=False):
    xin, qf, kfull, vfull, wdT, tri, xout = (
        io["xin"], io["qf"], io["kfull"], io["vfull"], io["wdT"], io["tri"],
        io["xout"])
    attn = ctx.sb("attn", 128, VT * T, BF16)
    with ctx.scope():
        trib = ctx.sb("trib", 128, 128, F32)
        idb = ctx.sb("idb", 128, 128, BF16)
        csem = ctx.name("c1a")
        ttok = ctx.dma("gpsimd", trib[:, :], tri[:, :], sem=csem)
        idtok = ctx.dma("gpsimd", idb[:, :], io["ident"][:, :], sem=csem)
        c2 = (csem, 32)

        ksb = [ctx.sb("ksb", 128, 2048, BF16) for _ in range(2)]
        vsb = [ctx.sb("vsb", 128, 16 * 128, BF16) for _ in range(2)]
        qsb = [ctx.sb("qsb", 128, 128, BF16) for _ in range(2)]
        Pb = ctx.sb("Pb", 128, 2048, BF16)
        pts = ctx.sb("pts", 128, 16 * 512, BF16)
        sc = ctx.sb("scm", 128, 16, F32)
        pssc = ctx.ps("pssc", 128, 2048)
        pstr = ctx.ps("pstr", 128, 128, BF16)
        psav = ctx.ps("psav", 128, 512)
        kvsem = [ctx.name("kv") for _ in range(2)]
        qsem = [ctx.name("qs") for _ in range(2)]
        kuse = [None, None]
        quse = [None, None]
        qcnt = 0
        avev = None
        scev = None
        avt = None
        sclap = ctx.constf(SCL)

        for kvh in range(8):
            sl = kvh % 2
            w = [kuse[sl]] if kuse[sl] else []
            ctx.dma("gpsimd", ksb[sl][:, :],
                    kfull[kvh * 128:(kvh + 1) * 128, :], waits=w, sem=kvsem[sl])
            for tc in range(16):
                ctx.dma("gpsimd", vsb[sl][:, tc * 128:(tc + 1) * 128],
                        vfull[tc * 128:(tc + 1) * 128,
                              kvh * 128:(kvh + 1) * 128], sem=kvsem[sl])
            nkv = 16 * 17 * (kvh // 2 + 1)
            for j in range(8):
                jj = moff + j
                L = (jj + 1) * 128
                ntc = jj + 1
                trev = None
                for g in range(4):
                    hq = kvh * 4 + g
                    qsl = qcnt % 2
                    w = [quse[qsl]] if quse[qsl] else []
                    ctx.dma("gpsimd", qsb[qsl][:, :],
                            qf[hq * 128:(hq + 1) * 128, j * 128:(j + 1) * 128],
                            waits=w, sem=qsem[qsl])
                    qcnt += 1
                    nq = 16 * ((qcnt - 1) // 2 + 1)
                    mmt = None
                    nch_sc = (L + 511) // 512
                    for c_ in range(nch_sc):
                        n = min(512, L - c_ * 512)
                        mmt = ctx.on("tensor",
                                     lambda e, sl=sl, qsl=qsl, c_=c_, n=n: e.matmul(
                                         pssc[:, c_ * 512:c_ * 512 + n],
                                         qsb[qsl][:, :],
                                         ksb[sl][:, c_ * 512:c_ * 512 + n],
                                         start=True, stop=True),
                                     waits=[(qsem[qsl], nq), (kvsem[sl], nkv),
                                            scev] if c_ == 0 else [],
                                     inc=("pc", 1) if c_ == nch_sc - 1 else None)
                    mk = ctx.on("vector", lambda e, jj=jj: e.tensor_add(
                        out=pssc[:, jj * 128:(jj + 1) * 128],
                        in0=pssc[:, jj * 128:(jj + 1) * 128],
                        in1=trib[:, :]), waits=[mmt, c2])
                    ctx.on("vector", lambda e, L=L: e.reduce_max(
                        out=sc[:, 0:1], in_=pssc[:, 0:L], axis=AX.X))
                    ng = ctx.on("vector", lambda e: e.tensor_scalar(
                        out=sc[:, 1:2], in0=sc[:, 0:1], scalar1=-SCL,
                        scalar2=None, op0=ALU.mult))
                    ex = ctx.on("scalar", lambda e, L=L: e.activation(
                        out=Pb[:, 0:L], in_=pssc[:, 0:L], func=AF.Exp,
                        bias=sc[:, 1:2], scale=sclap), waits=[ng])
                    ctx.on("vector", lambda e, L=L: e.reduce_sum(
                        out=sc[:, 2:3], in_=Pb[:, 0:L], axis=AX.X), waits=[ex])
                    ctx.on("vector", lambda e: e.reciprocal(
                        out=sc[:, 3:4], in_=sc[:, 2:3]))
                    scev = ctx.on("vector", lambda e, L=L: e.tensor_scalar(
                        out=Pb[:, 0:L], in0=Pb[:, 0:L], scalar1=sc[:, 3:4],
                        scalar2=None, op0=ALU.mult))
                    quse[qsl] = mmt
                    for tc in range(ntc):
                        tt = ctx.on("tensor", lambda e, tc=tc: e.transpose(
                            pstr[:, :], Pb[:, tc * 128:(tc + 1) * 128],
                            idb[:, :]),
                            waits=[scev, c2, trev] if tc == 0 else [trev],
                            inc=("pc", 1))
                        trev = ctx.on("vector",
                                      lambda e, tc=tc, g=g: e.tensor_copy(
                                          out=pts[:, tc * 512 + g * 128:
                                                  tc * 512 + (g + 1) * 128],
                                          in_=pstr[:, :]), waits=[tt])
                avt = None
                for tc in range(ntc):
                    avt = ctx.on("tensor", lambda e, sl=sl, tc=tc: e.matmul(
                        psav[:, :], vsb[sl][:, tc * 128:(tc + 1) * 128],
                        pts[:, tc * 512:(tc + 1) * 512],
                        start=(tc == 0), stop=(tc == ntc - 1)),
                        waits=[trev, avev, (kvsem[sl], nkv)] if tc == 0 else [],
                        inc=("pc", 1) if tc == ntc - 1 else None)
                for g in range(4):
                    hq = kvh * 4 + g
                    avev = ctx.on("vector",
                                  lambda e, g=g, hq=hq, j=j: e.tensor_copy(
                                      out=attn[:, hq * T + j * 128:
                                               hq * T + (j + 1) * 128],
                                      in_=psav[:, g * 128:(g + 1) * 128]),
                                  waits=[avt] if g == 0 else [])
            kuse[sl] = avt
        attdone = avev

    with ctx.scope():
        NS = 2
        wb = [ctx.sb("wdb", 128, 32 * 128, BF16) for _ in range(NS)]
        pso = [ctx.ps("psde", 128, 512) for _ in range(2)]
        xo = [ctx.sb("xo", 128, 512, F32) for _ in range(2)]
        xn = [ctx.sb("xn", 128, 512, F32) for _ in range(2)]
        wsem = [ctx.name("wd") for _ in range(NS)]
        xosem = [ctx.name("xoa") for _ in range(2)]
        stsem = [ctx.name("sta") for _ in range(2)]
        usetok = [None] * NS
        xotok = [None, None]
        evtok = [None, None]
        sttok = [None, None]
        for om in range(32):
            sl = om % NS
            w = [usetok[sl]] if usetok[sl] else []
            ctx.dma("gpsimd", wb[sl][:, :], wdT[om], waits=w, sem=wsem[sl])
            for tch in range(NCH):
                pi = (om * NCH + tch) % 2
                mmt = None
                for k in range(VT):
                    mmt = ctx.on("tensor", lambda e, sl=sl, k=k, tch=tch, pi=pi: e.matmul(
                        pso[pi][:, :], wb[sl][:, k * 128:(k + 1) * 128],
                        attn[:, k * T + tch * 512:k * T + (tch + 1) * 512],
                        start=(k == 0), stop=(k == VT - 1)),
                        waits=[(wsem[sl], 16 * (om // NS + 1)), attdone,
                               evtok[pi]] if k == 0 else [],
                        inc=("pc", 1) if k == VT - 1 else None)
                usetok[sl] = mmt
                xotok[pi] = ctx.dma("scalar", xo[pi][:, :],
                                    xin[om * 128:(om + 1) * 128,
                                        tch * 512:(tch + 1) * 512],
                                    waits=[evtok[pi]] if evtok[pi] else [],
                                    sem=xosem[pi])
                evtok[pi] = ctx.on("vector", lambda e, pi=pi: e.tensor_add(
                    out=xn[pi][:, :], in0=pso[pi][:, :], in1=xo[pi][:, :]),
                    waits=[mmt, xotok[pi], sttok[pi]])
                sttok[pi] = ctx.dma("sync",
                                    xout[om * 128:(om + 1) * 128,
                                         tch * 512:(tch + 1) * 512],
                                    xn[pi][:, :], waits=[evtok[pi]],
                                    sem=stsem[pi])
        ctx.on("sync", None, waits=[sttok[0], sttok[1]])
    if dbg:
        dt_ = ctx.dma("gpsimd", io["dbg_attn"].rearrange("(v p) t -> p v t", p=128),
                      attn.ap().rearrange("p (v t) -> p v t", v=VT),
                      waits=[ctx.tok("vc")])
        ctx.on("sync", None, waits=[dt_])


def build_l1a_att(moff, dbg=False):
    nc = bass.Bass("TRN2", target_bir_lowering=False)
    io = dict(
        xin=nc.dram_tensor("xin", [D, T], F32, kind="ExternalInput").ap(),
        qf=nc.dram_tensor("qf", [D, T], BF16, kind="ExternalInput").ap(),
        kfull=nc.dram_tensor("kfull", [1024, 2048], BF16, kind="ExternalInput").ap(),
        vfull=nc.dram_tensor("vfull", [2048, 1024], BF16, kind="ExternalInput").ap(),
        wdT=nc.dram_tensor("wdT", [32, 128, 32 * 128], F32, kind="ExternalInput").ap(),
        tri=nc.dram_tensor("tri", [128, 128], F32, kind="ExternalInput").ap(),
        ident=nc.dram_tensor("ident", [128, 128], F32, kind="ExternalInput").ap(),
        xout=nc.dram_tensor("xout", [D, T], F32, kind="ExternalOutput").ap(),
    )
    if dbg:
        io["dbg_attn"] = nc.dram_tensor("dbg_attn", [D, T], BF16,
                                        kind="ExternalOutput").ap()
    ctx = Ctx(nc)
    emit_l1a_att_m(ctx, io, moff, dbg=dbg)
    ctx.finish()
    return nc


def stage_l1a_common(inp):
    w_qkv = np.asarray(inp["L1_attn_w_qkv"])
    w_dense = np.asarray(inp["L1_attn_w_dense"])
    wqkT = np.ascontiguousarray(w_qkv[:5120].T)
    wvT = np.ascontiguousarray(w_qkv[5120:].T)
    tri = np.where(np.arange(128)[:, None] >= np.arange(128)[None, :],
                   0.0, -1e4).astype(np.float32)
    return dict(
        wqk=tile_img_s(wqkT, 40, 32, 128),
        wv=np.ascontiguousarray(wvT.reshape(32, 128, 1024)
                                .transpose(1, 0, 2).reshape(128, 32 * 1024)),
        wdT=tile_img_s(w_dense.T, 32, 32, 128),
        tri=tri,
        ident=np.eye(128, dtype=np.float32),
    )


DT4 = 4
TWOPI = 6.283185307179586
PI = 3.141592653589793


def emit_ln_feat(ctx, src_bf, KT, ones_bf, deps, bcb, x2buf=None,
                 rows_buf=None):
    with ctx.scope():
        btok = _emit_ln_feat_body(ctx, src_bf, KT, ones_bf, deps, bcb, x2buf,
                                  rows_buf)
    return btok


def _emit_ln_feat_body(ctx, src_bf, KT, ones_bf, deps, bcb, x2buf, rows_buf):
    ps_s = ctx.ps("lfs", 128, 1024)
    ps_q = ctx.ps("lfq", 128, 1024)
    ps_bc = ctx.ps("lfb", 128, 512)
    NX2 = KT
    x2 = x2buf if x2buf is not None else ctx.sb("lfx2", 128, NX2 * T, BF16)
    rows = rows_buf if rows_buf is not None else ctx.sb("lfrow", 1, 4 * T, F32)
    rbf = ctx.sb("lfrbf", 1, 2 * T, BF16)
    DIMF = KT * 128

    sqt = [None] * KT
    x2mm = [None] * KT
    mtok = None
    for v in range(KT):
        w = list(deps)
        if v >= NX2:
            w.append(x2mm[v - NX2])
        sqt[v] = ctx.on("scalar", lambda e, v=v: e.activation(
            out=x2[:, (v % NX2) * T:(v % NX2 + 1) * T],
            in_=src_bf[:, v * T:(v + 1) * T], func=AF.Square), waits=w)
    for v in range(KT):
        for c in range(NCH):
            ctx.on("tensor", lambda e, v=v, c=c: e.matmul(
                ps_s[0:1, c * 512:(c + 1) * 512], ones_bf[:, 0:1],
                src_bf[:, v * T + c * 512:v * T + (c + 1) * 512],
                start=(v == 0), stop=(v == KT - 1)), waits=list(deps))
            mtok = ctx.on("tensor", lambda e, v=v, c=c: e.matmul(
                ps_q[0:1, c * 512:(c + 1) * 512], ones_bf[:, 0:1],
                x2[:, (v % NX2) * T + c * 512:(v % NX2) * T + (c + 1) * 512],
                start=(v == 0), stop=(v == KT - 1)), waits=[sqt[v]],
                inc=("pc", 1))
        x2mm[v] = mtok
    ctx.on("vector", lambda e: e.tensor_scalar(
        out=rows[:, 0:T], in0=ps_s[0:1, 0:T], scalar1=1.0 / DIMF, scalar2=None,
        op0=ALU.mult), waits=[mtok])
    ctx.on("vector", lambda e: e.tensor_scalar(
        out=rows[:, T:2 * T], in0=ps_q[0:1, 0:T], scalar1=1.0 / DIMF,
        scalar2=None, op0=ALU.mult))
    ctx.on("vector", lambda e: e.tensor_mul(
        out=rows[:, 2 * T:3 * T], in0=rows[:, 0:T], in1=rows[:, 0:T]))
    ctx.on("vector", lambda e: e.tensor_sub(
        out=rows[:, 2 * T:3 * T], in0=rows[:, T:2 * T], in1=rows[:, 2 * T:3 * T]))
    vtok = ctx.on("vector", lambda e: e.tensor_scalar(
        out=rows[:, 2 * T:3 * T], in0=rows[:, 2 * T:3 * T], scalar1=EPS,
        scalar2=None, op0=ALU.add))
    atok = ctx.on("scalar", lambda e: e.activation(
        out=rows[:, 3 * T:4 * T], in_=rows[:, 2 * T:3 * T], func=AF.Sqrt),
        waits=[vtok])
    ctx.on("vector", lambda e: e.reciprocal(
        out=rows[:, 2 * T:3 * T], in_=rows[:, 3 * T:4 * T]), waits=[atok])
    ctx.on("vector", lambda e: e.tensor_mul(
        out=rows[:, 3 * T:4 * T], in0=rows[:, 0:T], in1=rows[:, 2 * T:3 * T]))
    rtok = ctx.on("vector", lambda e: e.tensor_copy(
        out=rbf[:, :], in_=rows[:, 2 * T:4 * T]))
    btok = None
    for half in range(2):
        for c in range(NCH):
            tk = ctx.on("tensor", lambda e, half=half, c=c: e.matmul(
                ps_bc[:, :], ones_bf[0:1, 4:132],
                rbf[:, half * T + c * 512: half * T + (c + 1) * 512],
                start=True, stop=True),
                waits=[rtok] + ([btok] if btok else []), inc=("pc", 1))
            btok = ctx.on("scalar", lambda e, half=half, c=c: e.activation(
                out=bcb[:, half * T + c * 512: half * T + (c + 1) * 512],
                in_=ps_bc[:, :], func=AF.Identity), waits=[tk])
    if getattr(ctx, "_lnrow_dump", None) is not None:
        dst = ctx._lnrow_dump
        ctx._lnrow_dump = None
        ctx.on("vector", lambda e: e.tensor_copy(out=dst, in_=rows[:, 0:2 * T]))
    return btok


def emit_l0a(ctx, io, mode):
    nc = ctx.nc
    ones, onetok = emit_ones(ctx)
    one_f = ctx.constf(1.0)
    piap = ctx.constf(PI)

    h = ctx.sb("h", 128, DT4 * T, F32)
    hbf = ctx.sb("hbf", 128, DT4 * T, BF16)
    deps_h = []
    prep = mode in ("A", 0, 1, 2)
    fin = mode != "A"
    lay_p = 0 if mode == "A" else mode + 1

    if mode == "A":
        with ctx.scope():
            xln = ctx.sb("xlnA", 128, VT * T, BF16)
            lnall = emit_ln_outer(ctx, io["x0"], xln, ones, onetok)
            wb = [ctx.sb("wbA", 128, 32 * 128, BF16) for _ in range(2)]
            binb = ctx.sb("binb", 128, 4, F32)
            bsem = ctx.name("binA")
            bintok = ctx.dma("gpsimd", binb[:, :], io["bin"][:, :], sem=bsem)
            psw = ctx.ps("pswA", 128, 1024)
            wsem = [ctx.name("wA") for _ in range(2)]
            use = [None, None]
            ev = None
            for om in range(DT4):
                sl = om % 2
                w = [use[sl]] if use[sl] else []
                ctx.dma("gpsimd", wb[sl][:, :], io["win"][om], waits=w,
                        sem=wsem[sl])
                mmt = None
                for tch in range(NCH):
                    for k in range(VT):
                        mmt = ctx.on("tensor",
                                     lambda e, sl=sl, k=k, tch=tch: e.matmul(
                                         psw[:, tch * 512:(tch + 1) * 512],
                                         wb[sl][:, k * 128:(k + 1) * 128],
                                         xln[:, k * T + tch * 512:k * T + (tch + 1) * 512],
                                         start=(k == 0), stop=(k == VT - 1)),
                                     waits=[(wsem[sl], 16 * (om // 2 + 1)),
                                            lnall, ev] if k == 0 and tch == 0
                                     else [],
                                     inc=("pc", 1) if k == VT - 1 and
                                     tch == NCH - 1 else None)
                use[sl] = mmt
                ev = ctx.on("scalar", lambda e, om=om: e.activation(
                    out=h[:, om * T:(om + 1) * T], in_=psw[:, 0:T],
                    func=AF.Identity, bias=binb[:, om:om + 1], scale=1.0),
                    waits=[mmt, bintok])
            deps_h = [ev]

    with ctx.scope():
        bcbS = ctx.sb("bcbS", 128, 2 * T, BF16)
        ctx._lnx2buf = ctx.sb("lnx2S", 128, DT4 * T, BF16)
        xh = hbf
        cs_t = ctx.sb("cs_t", 128, DT4 * T, F32)
        ctx._lnrowbuf = cs_t[0:1, 0:4 * T]
        sn_t = ctx.sb("sn_t", 128, DT4 * T, F32)
        wre = ctx.sb("wre", 128, DT4 * T, F32)
        wim = ctx.sb("wim", 128, DT4 * T, F32)
        ure = ctx.sb("ure", 128, DT4 * T, F32)
        uim = ctx.sb("uim", 128, DT4 * T, F32)
        y = wre
        ph2 = ure[:, 0:T]
        rootp = ure[:, T:2 * T]
        kfp = uim[:, 0:T]
        kint = ctx.sb("kint", 128, T, I32)
        t_idx = ctx.sb("t_idx", 128, T, F32)
        cc = ctx.sb("cc", 128, 48, F32)
        pl = ctx.sb("pl", 128, 36, F32)
        stt = ctx.sb("stt", 128, 16, F32)
        sti = ctx.sb("sti", 128, 8, F32)
        t0r = ctx.sb("t0r", 128, 1, F32)
        csem = ctx.name("l0c")

        t0tok = ctx.dma("gpsimd", t0r[:, :], io["t0row"][:, :], sem=csem)
        titok = ctx.on("vector", lambda e: e.tensor_tensor_scan(
            out=t_idx[:, :], data0=bass.broadcast_tensor_aps(one_f, t_idx[:, :])[0],
            data1=bass.broadcast_tensor_aps(one_f, t_idx[:, :])[0],
            initial=t0r[:, 0:1], op0=ALU.mult, op1=ALU.add), waits=[t0tok])

        if fin:
            hsem = ctx.name("hld")
            ht = ctx.dma("gpsimd", h[:, :], io["h_in"][:, :], sem=hsem)
            deps_h = [ht]

        if fin:
            wsem2 = ctx.name("wtb")
            wt1 = ctx.dma("gpsimd", wre[:, :], io["w_in"][:, 0:4 * T], sem=wsem2)
            ctx.dma("gpsimd", wim[:, :], io["w_in"][:, 4 * T:8 * T], sem=wsem2)
            ctx.dma("gpsimd", cs_t[:, :], io["tb_in"][:, 0:4 * T], sem=wsem2)
            ctx.dma("gpsimd", sn_t[:, :], io["tb_in"][:, 4 * T:8 * T], sem=wsem2)
            ctx.dma("gpsimd", sti[:, :], io["st_in"][:, :], sem=wsem2)
            ctx.dma("gpsimd", pl[:, :], io["plf"][:, :], sem=wsem2)
            wld = (wsem2, 16 * 6)
            rrtok = emit_l0a_rr(ctx, pl, cc, [wld])
            scux = []
            for dt in range(DT4):
                rb = bass.broadcast_tensor_aps(cc[:, 0 + dt:1 + dt],
                                               ure[:, dt * T:(dt + 1) * T])[0]
                ctx.on("vector", lambda e, dt=dt, rb=rb: e.tensor_tensor_scan(
                    out=ure[:, dt * T:(dt + 1) * T], data0=rb,
                    data1=wre[:, dt * T:(dt + 1) * T],
                    initial=sti[:, dt:dt + 1], op0=ALU.mult, op1=ALU.add),
                    waits=[rrtok, wld])
                ctx.on("vector", lambda e, dt=dt, rb=rb: e.tensor_tensor_scan(
                    out=uim[:, dt * T:(dt + 1) * T], data0=rb,
                    data1=wim[:, dt * T:(dt + 1) * T],
                    initial=sti[:, 4 + dt:5 + dt], op0=ALU.mult, op1=ALU.add))
                ctx.on("vector", lambda e, dt=dt: e.tensor_mul(
                    out=wre[:, dt * T:(dt + 1) * T],
                    in0=cs_t[:, dt * T:(dt + 1) * T],
                    in1=ure[:, dt * T:(dt + 1) * T]))
                ctx.on("vector", lambda e, dt=dt: e.tensor_mul(
                    out=wim[:, dt * T:(dt + 1) * T],
                    in0=sn_t[:, dt * T:(dt + 1) * T],
                    in1=uim[:, dt * T:(dt + 1) * T]))
                ctx.on("vector", lambda e, dt=dt: e.tensor_sub(
                    out=wre[:, dt * T:(dt + 1) * T],
                    in0=wre[:, dt * T:(dt + 1) * T],
                    in1=wim[:, dt * T:(dt + 1) * T]))
                scux.append(ctx.on("vector", lambda e, dt=dt: e.scalar_tensor_tensor(
                    out=h[:, dt * T:(dt + 1) * T],
                    in0=wre[:, dt * T:(dt + 1) * T],
                    scalar=pl[:, 16 + dt:17 + dt],
                    in1=h[:, dt * T:(dt + 1) * T],
                    op0=ALU.mult, op1=ALU.add), waits=deps_h))
            if io.get("hmid") is not None:
                dmh = ctx.dma("gpsimd", io["hmid"], h[:, :], waits=scux)
                dmu = ctx.dma("gpsimd", io["udbg"], ure[:, :],
                              waits=[ctx.tok("vc")])
                ctx._zdump = True
            hb2 = ctx.on("vector", lambda e: e.tensor_copy(
                out=hbf[:, :], in_=h[:, :]), waits=scux)
            bt2 = emit_ln_feat(ctx, hbf, DT4, ones, [hb2], bcbS, ctx._lnx2buf, ctx._lnrowbuf)
            bcb2 = bcbS
            xtoks = []
            for dt in range(DT4):
                ctx.on("vector", lambda e, dt=dt: e.tensor_mul(
                    out=xh[:, dt * T:(dt + 1) * T],
                    in0=hbf[:, dt * T:(dt + 1) * T], in1=bcb2[:, 0:T]),
                    waits=[bt2])
                xt2 = ctx.on("vector", lambda e, dt=dt: e.tensor_sub(
                    out=xh[:, dt * T:(dt + 1) * T],
                    in0=xh[:, dt * T:(dt + 1) * T], in1=bcb2[:, T:2 * T]))
                xtoks.append(ctx.on("scalar", lambda e, dt=dt: e.activation(
                    out=xh[:, dt * T:(dt + 1) * T],
                    in_=xh[:, dt * T:(dt + 1) * T], func=AF.Identity,
                    bias=pl[:, 12 + dt:13 + dt],
                    scale=pl[:, 8 + dt:9 + dt]), waits=[xt2, wld]))
            z = ctx.sb("z", 128, 16 * T, BF16)
            w1b = [ctx.sb("w1b", 128, 512, BF16) for _ in range(2)]
            b1b = ctx.sb("b1b", 128, 16, F32)
            fsem = ctx.name("ffn")
            b1tok = ctx.dma("gpsimd", b1b[:, :], io["b1"][:, :], sem=fsem)
            w1sem = [ctx.name("w1") for _ in range(2)]
            psf = ctx.ps("psf", 128, 1024)
            use = [None, None]
            ev = None
            for om in range(16):
                sl = om % 2
                w = [use[sl]] if use[sl] else []
                ctx.dma("gpsimd", w1b[sl][:, :], io["w1"][om], waits=w,
                        sem=w1sem[sl])
                mmt = None
                for tch in range(NCH):
                    for k in range(DT4):
                        mmt = ctx.on("tensor",
                                     lambda e, sl=sl, k=k, tch=tch: e.matmul(
                                         psf[:, tch * 512:(tch + 1) * 512],
                                         w1b[sl][:, k * 128:(k + 1) * 128],
                                         xh[:, k * T + tch * 512:k * T + (tch + 1) * 512],
                                         start=(k == 0), stop=(k == DT4 - 1)),
                                     waits=[(w1sem[sl], 16 * (om // 2 + 1)),
                                            xtoks[-1], ev] if k == 0 and
                                     tch == 0 else [],
                                     inc=("pc", 1) if k == DT4 - 1 and
                                     tch == NCH - 1 else None)
                use[sl] = mmt
                ev = ctx.on("scalar", lambda e, om=om: e.activation(
                    out=z[:, om * T:(om + 1) * T], in_=psf[:, 0:T],
                    func=AF.Silu, bias=b1b[:, om:om + 1], scale=1.0),
                    waits=[mmt, b1tok])
            if getattr(ctx, "_zdump", False):
                ctx.dma("gpsimd", io["udbg"], z[:, 0:4 * T],
                        waits=[ev, ctx.tok("vc")])
            w2b = [ctx.sb("w2b", 128, 2048, BF16) for _ in range(2)]
            b2b = ctx.sb("b2b", 128, 4, F32)
            b2tok = ctx.dma("gpsimd", b2b[:, :], io["b2"][:, :], sem=fsem)
            w2sem = [ctx.name("w2") for _ in range(2)]
            use = [None, None]
            hev = None
            for om in range(DT4):
                sl = om % 2
                w = [use[sl]] if use[sl] else []
                ctx.dma("gpsimd", w2b[sl][:, :], io["w2"][om], waits=w,
                        sem=w2sem[sl])
                mmt = None
                for tch in range(NCH):
                    for k in range(16):
                        mmt = ctx.on("tensor",
                                     lambda e, sl=sl, k=k, tch=tch: e.matmul(
                                         psf[:, tch * 512:(tch + 1) * 512],
                                         w2b[sl][:, k * 128:(k + 1) * 128],
                                         z[:, k * T + tch * 512:k * T + (tch + 1) * 512],
                                         start=(k == 0), stop=(k == 15)),
                                     waits=[(w2sem[sl], 16 * (om // 2 + 1)),
                                            ev, hev] if k == 0 and tch == 0
                                     else [],
                                     inc=("pc", 1) if k == 15 and
                                     tch == NCH - 1 else None)
                use[sl] = mmt
                hev = ctx.on("vector", lambda e, om=om: e.scalar_tensor_tensor(
                    out=h[:, om * T:(om + 1) * T], in0=psf[:, 0:T],
                    scalar=b2b[:, om:om + 1], in1=h[:, om * T:(om + 1) * T],
                    op0=ALU.add, op1=ALU.add), waits=[mmt, b2tok])
            deps_h = [hev]

        if prep:
            pl2 = ctx.sb("pl2", 128, 36, F32)
            pltok = ctx.dma("gpsimd", pl2[:, :], io["plp"][:, :], sem=csem)
            ctx._cc_dump, ctx._pl_dump = cc, pl2
            if io.get("tdbg") is not None:
                ctx._tdbg = ctx.sb("tdbgb", 128, 4 * T, F32)
            emit_l0a_prep(ctx, io, h, hbf, xh, y, cs_t, sn_t, wre, wim,
                          ure, uim, ph2, rootp, kint, kfp, t_idx, titok,
                          cc, pl2, pltok, stt, ones, deps_h, piap, bcbS)
        elif mode == 3:
            lnlb = ctx.sb("lnlb", 128, 8, F32)
            lnltok = ctx.dma("gpsimd", lnlb[:, :], io["lnl"][:, :], sem=csem)
            hb3 = ctx.on("vector", lambda e: e.tensor_copy(
                out=hbf[:, :], in_=h[:, :]), waits=deps_h)
            bt3 = emit_ln_feat(ctx, hbf, DT4, ones, [hb3], bcbS, ctx._lnx2buf, ctx._lnrowbuf)
            bcb3 = bcbS
            ytoks = []
            for dt in range(DT4):
                ctx.on("vector", lambda e, dt=dt: e.tensor_mul(
                    out=xh[:, dt * T:(dt + 1) * T],
                    in0=hbf[:, dt * T:(dt + 1) * T], in1=bcb3[:, 0:T]),
                    waits=[bt3])
                xt = ctx.on("vector", lambda e, dt=dt: e.tensor_sub(
                    out=xh[:, dt * T:(dt + 1) * T],
                    in0=xh[:, dt * T:(dt + 1) * T], in1=bcb3[:, T:2 * T]))
                ytoks.append(ctx.on("scalar", lambda e, dt=dt: e.activation(
                    out=xh[:, dt * T:(dt + 1) * T],
                    in_=xh[:, dt * T:(dt + 1) * T], func=AF.Identity,
                    bias=lnlb[:, 4 + dt:5 + dt], scale=lnlb[:, dt:dt + 1]),
                    waits=[xt, lnltok]))
            wob = [ctx.sb("wob", 128, 512, BF16) for _ in range(2)]
            bob = ctx.sb("bob", 128, 32, F32)
            osem = ctx.name("wo")
            botok = ctx.dma("gpsimd", bob[:, :], io["bout"][:, :], sem=osem)
            wosem = [ctx.name("wos") for _ in range(2)]
            pso = [ctx.ps("psoO", 128, 512) for _ in range(2)]
            xo = [ctx.sb("xoO", 128, 512, F32) for _ in range(2)]
            xn = [ctx.sb("xnO", 128, 512, F32) for _ in range(2)]
            xosem = [ctx.name("xoo") for _ in range(2)]
            stsem = [ctx.name("sto") for _ in range(2)]
            use = [None, None]
            xotok = [None, None]
            evtok = [None, None]
            sttok = [None, None]
            for om in range(32):
                sl = om % 2
                w = [use[sl]] if use[sl] else []
                ctx.dma("gpsimd", wob[sl][:, :], io["wout"][om], waits=w,
                        sem=wosem[sl])
                for tch in range(NCH):
                    pi = (om * NCH + tch) % 2
                    mmt = None
                    for k in range(DT4):
                        mmt = ctx.on("tensor",
                                     lambda e, sl=sl, k=k, tch=tch, pi=pi: e.matmul(
                                         pso[pi][:, :],
                                         wob[sl][:, k * 128:(k + 1) * 128],
                                         xh[:, k * T + tch * 512:k * T + (tch + 1) * 512],
                                         start=(k == 0), stop=(k == DT4 - 1)),
                                     waits=[(wosem[sl], 16 * (om // 2 + 1)),
                                            ytoks[-1], evtok[pi]] if k == 0
                                     else [],
                                     inc=("pc", 1) if k == DT4 - 1 else None)
                    use[sl] = mmt
                    xotok[pi] = ctx.dma("scalar", xo[pi][:, :],
                                        io["x0"][om * 128:(om + 1) * 128,
                                                 tch * 512:(tch + 1) * 512],
                                        waits=[evtok[pi]] if evtok[pi] else [],
                                        sem=xosem[pi])
                    ev2 = ctx.on("vector", lambda e, om=om, pi=pi: e.scalar_tensor_tensor(
                        out=xn[pi][:, :], in0=pso[pi][:, :],
                        scalar=bob[:, om:om + 1], in1=xo[pi][:, :],
                        op0=ALU.add, op1=ALU.add),
                        waits=[mmt, xotok[pi], botok, sttok[pi]])
                    evtok[pi] = ev2
                    sttok[pi] = ctx.dma("sync",
                                        io["xout"][om * 128:(om + 1) * 128,
                                                   tch * 512:(tch + 1) * 512],
                                        xn[pi][:, :], waits=[ev2],
                                        sem=stsem[pi])
            ctx.on("sync", None, waits=[sttok[0], sttok[1]])

        osem2 = ctx.name("outs")
        fintoks = []
        if mode != 3:
            fintoks.append(ctx.dma("sync", io["h_out"][:, :], h[:, :],
                                   waits=[ctx.tok("vc"), ctx.tok("ac")],
                                   sem=osem2))
        if prep:
            fintoks.append(ctx.dma("sync", io["w_out"][:, 0:4 * T], wre[:, :],
                                   waits=[ctx.tok("vc")], sem=osem2))
            fintoks.append(ctx.dma("sync", io["w_out"][:, 4 * T:8 * T],
                                   wim[:, :], waits=[ctx.tok("vc")], sem=osem2))
            fintoks.append(ctx.dma("sync", io["tb_out"][:, 0:4 * T], cs_t[:, :],
                                   waits=[ctx.tok("vc")], sem=osem2))
            fintoks.append(ctx.dma("sync", io["tb_out"][:, 4 * T:8 * T],
                                   sn_t[:, :], waits=[ctx.tok("vc")], sem=osem2))
            fintoks.append(ctx.dma("sync", io["st_out"][:, :], stt[:, :],
                                   waits=[ctx.tok("vc")], sem=osem2))
        if io.get("ccout") is not None and prep:
            dcc = ctx.dma("gpsimd", io["ccout"][:, 0:48], ctx._cc_dump[:, :],
                          waits=[ctx.tok("vc"), ctx.tok("ac")])
            dcc = ctx.dma("gpsimd", io["ccout"][:, 48:84], ctx._pl_dump[:, :])
            if getattr(ctx, "_tdbg", None) is not None:
                dcc = ctx.dma("gpsimd", io["tdbg"], ctx._tdbg[:, :])
            ctx.on("sync", None, waits=[dcc])
        if fintoks:
            ctx.on("sync", None, waits=[fintoks[-1]])


def emit_l0a_rr(ctx, pl, cc, deps):
    t1 = ctx.on("vector", lambda e: e.tensor_mul(
        out=cc[:, 28:32], in0=pl[:, 20:24], in1=pl[:, 20:24]), waits=list(deps))
    ctx.on("vector", lambda e: e.tensor_mul(
        out=cc[:, 32:36], in0=pl[:, 24:28], in1=pl[:, 24:28]))
    t2 = ctx.on("vector", lambda e: e.tensor_add(
        out=cc[:, 28:32], in0=cc[:, 28:32], in1=cc[:, 32:36]))
    t3 = ctx.on("scalar", lambda e: e.activation(
        out=cc[:, 28:32], in_=cc[:, 28:32], func=AF.Sqrt), waits=[t2])
    return ctx.on("scalar", lambda e: e.activation(
        out=cc[:, 0:4], in_=cc[:, 28:32], func=AF.Tanh))


def emit_l0a_prep(ctx, io, h, hbf, xh, y, cs_t, sn_t, wre, wim, ure, uim,
                  ph2, rootp, kint, kfp, t_idx, titok, cc, pl, pltok, stt,
                  ones, deps_h, piap, bcb):
    hb = ctx.on("vector", lambda e: e.tensor_copy(out=hbf[:, :], in_=h[:, :]),
                waits=deps_h)
    if getattr(ctx, "_tdbg", None) is not None:
        ctx._lnrow_dump = ctx._tdbg[0:1, 2 * T:4 * T]
    btok = emit_ln_feat(ctx, hbf, DT4, ones, [hb], bcb, ctx._lnx2buf, ctx._lnrowbuf)
    ytoks = []
    for dt in range(DT4):
        ctx.on("vector", lambda e, dt=dt: e.tensor_mul(
            out=xh[:, dt * T:(dt + 1) * T], in0=hbf[:, dt * T:(dt + 1) * T],
            in1=bcb[:, 0:T]), waits=[btok])
        xt = ctx.on("vector", lambda e, dt=dt: e.tensor_sub(
            out=xh[:, dt * T:(dt + 1) * T], in0=xh[:, dt * T:(dt + 1) * T],
            in1=bcb[:, T:2 * T]))
        ytoks.append(ctx.on("scalar", lambda e, dt=dt: e.activation(
            out=y[:, dt * T:(dt + 1) * T], in_=xh[:, dt * T:(dt + 1) * T],
            func=AF.Silu, bias=pl[:, 4 + dt:5 + dt], scale=pl[:, dt:dt + 1]),
            waits=[xt, pltok]))

    C = lambda i: cc[:, 4 * i:4 * i + 4]
    halfap = ctx.constf(0.5)
    zap = ctx.constf(0.0)
    rrtok = emit_l0a_rr(ctx, pl, cc, [pltok])
    ctx.on("vector", lambda e: e.reciprocal(out=C(7), in_=pl[:, 20:24]),
           waits=[pltok, rrtok])
    q1 = ctx.on("vector", lambda e: e.tensor_mul(out=C(7), in0=pl[:, 24:28],
                                                 in1=C(7)))
    a0 = ctx.on("scalar", lambda e: e.activation(out=C(7), in_=C(7),
                                                 func=AF.Arctan), waits=[q1])
    sg = ctx.on("scalar", lambda e: e.activation(out=C(8), in_=pl[:, 24:28],
                                                 func=AF.Sign),
                waits=[pltok, rrtok])
    mk = ctx.on("vector", lambda e: e.tensor_scalar(
        out=C(9), in0=pl[:, 20:24], scalar1=0.0, scalar2=None, op0=ALU.is_lt),
        waits=[pltok])
    m2tok = ctx.on("vector", lambda e: e.tensor_mul(out=C(9), in0=C(9),
                                                    in1=C(8)), waits=[sg])
    th = ctx.on("vector", lambda e: e.scalar_tensor_tensor(
        out=C(9), in0=C(9), scalar=PI, in1=C(7), op0=ALU.mult, op1=ALU.add),
        waits=[a0])
    sg2 = ctx.on("scalar", lambda e: e.activation(out=C(2), in_=C(9),
                                                  func=AF.Sign), waits=[th])
    ctx.on("vector", lambda e: e.tensor_scalar(
        out=C(2), in0=C(2), scalar1=2.0, scalar2=None, op0=ALU.mult),
        waits=[sg2])
    ab = ctx.on("scalar", lambda e: e.activation(out=C(1), in_=C(9),
                                                 func=AF.Abs), waits=[th])
    c1tok = ctx.on("vector", lambda e: e.tensor_scalar(
        out=C(1), in0=C(1), scalar1=1.0 / TWOPI, scalar2=None, op0=ALU.mult),
        waits=[ab])
    sh = ctx.on("scalar", lambda e: e.activation(
        out=C(10), in_=C(9), func=AF.Sin, bias=zap, scale=halfap),
        waits=[th])
    s2 = ctx.on("scalar", lambda e: e.activation(out=C(8), in_=C(10),
                                                 func=AF.Square),
                waits=[m2tok])
    c11ts = ctx.on("vector", lambda e: e.tensor_scalar(
        out=C(11), in0=C(8), scalar1=-1.0, scalar2=1.0, op0=ALU.mult,
        op1=ALU.add), waits=[s2])
    rt = ctx.on("scalar", lambda e: e.activation(out=C(11), in_=C(11),
                                                 func=AF.Sqrt), waits=[c11ts])
    ctx.on("vector", lambda e: e.tensor_scalar(
        out=C(3), in0=C(8), scalar1=-2.0, scalar2=1.0, op0=ALU.mult,
        op1=ALU.add))
    snth = ctx.on("vector", lambda e: e.scalar_tensor_tensor(
        out=C(4), in0=C(10), scalar=2.0, in1=C(11), op0=ALU.mult,
        op1=ALU.mult), waits=[rt])
    ctx.on("vector", lambda e: e.tensor_mul(out=C(5), in0=pl[:, 28:32],
                                            in1=C(3)))
    ctx.on("vector", lambda e: e.tensor_mul(out=C(7), in0=pl[:, 32:36],
                                            in1=C(4)))
    ctx.on("vector", lambda e: e.tensor_sub(out=C(5), in0=C(5), in1=C(7)))
    ctx.on("vector", lambda e: e.tensor_mul(out=C(6), in0=pl[:, 32:36],
                                            in1=C(3)))
    ctx.on("vector", lambda e: e.tensor_mul(out=C(7), in0=pl[:, 28:32],
                                            in1=C(4)))
    lct = ctx.on("vector", lambda e: e.tensor_add(out=C(6), in0=C(6),
                                                  in1=C(7)))

    tdbg = ctx.nc and getattr(ctx, "_tdbg", None)
    for dt in range(DT4):
        sl_t = slice(dt * T, (dt + 1) * T)
        ctx.on("vector", lambda e, dt=dt: e.tensor_scalar(
            out=ph2, in0=t_idx[:, :], scalar1=cc[:, 4 + dt:5 + dt],
            scalar2=None, op0=ALU.mult), waits=[c1tok, titok])
        if dt == 0 and tdbg is not None:
            ctx.on("vector", lambda e: e.tensor_copy(out=tdbg[:, 0:T],
                                                     in_=y[:, 0:T]),
                   waits=[ytoks[0]])
        ctx.on("vector", lambda e: e.tensor_copy(out=kint[:, :], in_=ph2))
        ctx.on("vector", lambda e: e.tensor_copy(out=kfp, in_=kint[:, :]))
        sb2 = ctx.on("vector", lambda e: e.tensor_sub(out=ph2, in0=ph2,
                                                      in1=kfp))
        if dt == 0 and tdbg is not None:
            ctx.on("vector", lambda e: e.tensor_copy(out=tdbg[:, T:2 * T],
                                                     in_=bcb[:, 0:T]))
            ctx.on("vector", lambda e: e.tensor_copy(out=tdbg[:, 3 * T:4 * T],
                                                     in_=bcb[:, T:2 * T]))
        sh2 = ctx.on("scalar", lambda e, sl_t=sl_t: e.activation(
            out=sn_t[:, sl_t], in_=ph2, func=AF.Sin, bias=zap, scale=piap),
            waits=[sb2])

        sq2 = ctx.on("scalar", lambda e, sl_t=sl_t: e.activation(
            out=cs_t[:, sl_t], in_=sn_t[:, sl_t], func=AF.Square))
        rpts = ctx.on("vector", lambda e, sl_t=sl_t: e.tensor_scalar(
            out=rootp, in0=cs_t[:, sl_t], scalar1=-1.0, scalar2=1.0,
            op0=ALU.mult, op1=ALU.add), waits=[sq2])
        rt2 = ctx.on("scalar", lambda e: e.activation(
            out=rootp, in_=rootp, func=AF.Sqrt), waits=[rpts])
        ctx.on("vector", lambda e, sl_t=sl_t: e.tensor_scalar(
            out=cs_t[:, sl_t], in0=cs_t[:, sl_t], scalar1=-2.0, scalar2=1.0,
            op0=ALU.mult, op1=ALU.add))
        ctx.on("vector", lambda e, dt=dt, sl_t=sl_t: e.scalar_tensor_tensor(
            out=sn_t[:, sl_t], in0=sn_t[:, sl_t],
            scalar=cc[:, 8 + dt:9 + dt], in1=rootp, op0=ALU.mult,
            op1=ALU.mult), waits=[rt2])
        ctx.on("vector", lambda e, dt=dt, sl_t=sl_t: e.scalar_tensor_tensor(
            out=wim[:, sl_t], in0=sn_t[:, sl_t], scalar=-1.0, in1=y[:, sl_t],
            op0=ALU.mult, op1=ALU.mult), waits=[ytoks[dt]])
        ctx.on("vector", lambda e, dt=dt, sl_t=sl_t: e.tensor_mul(
            out=wre[:, sl_t], in0=cs_t[:, sl_t], in1=y[:, sl_t]))
    for dt in range(DT4):
        sl_t = slice(dt * T, (dt + 1) * T)
        rb = bass.broadcast_tensor_aps(cc[:, dt:dt + 1], ure[:, sl_t])[0]
        ctx.on("vector", lambda e, rb=rb, sl_t=sl_t, dt=dt: e.tensor_tensor_scan(
            out=ure[:, sl_t], data0=rb, data1=wre[:, sl_t],
            initial=cc[:, 20 + dt:21 + dt], op0=ALU.mult, op1=ALU.add),
            waits=[lct])
        ctx.on("vector", lambda e, rb=rb, sl_t=sl_t, dt=dt: e.tensor_tensor_scan(
            out=uim[:, sl_t], data0=rb, data1=wim[:, sl_t],
            initial=cc[:, 24 + dt:25 + dt], op0=ALU.mult, op1=ALU.add))
        ctx.on("vector", lambda e, dt=dt: e.tensor_copy(
            out=stt[:, dt:dt + 1], in_=cc[:, 20 + dt:21 + dt]))
        ctx.on("vector", lambda e, dt=dt: e.tensor_copy(
            out=stt[:, 4 + dt:5 + dt], in_=cc[:, 24 + dt:25 + dt]))
        ctx.on("vector", lambda e, dt=dt: e.tensor_copy(
            out=stt[:, 8 + dt:9 + dt],
            in_=ure[:, dt * T + T - 1:dt * T + T]))
        ctx.on("vector", lambda e, dt=dt: e.tensor_copy(
            out=stt[:, 12 + dt:13 + dt],
            in_=uim[:, dt * T + T - 1:dt * T + T]))


def build_l0a(mode, dbg=False):
    nc = bass.Bass("TRN2", target_bir_lowering=False)
    io = dict(t0row=nc.dram_tensor("t0row", [128, 1], F32,
                                   kind="ExternalInput").ap())
    if dbg:
        io["ccout"] = nc.dram_tensor("ccout", [128, 48 + 36], F32,
                                     kind="ExternalOutput").ap()
        io["tdbg"] = nc.dram_tensor("tdbg", [128, 4 * T], F32,
                                    kind="ExternalOutput").ap()
        io["hmid"] = nc.dram_tensor("hmid", [128, 4 * T], F32,
                                    kind="ExternalOutput").ap()
        io["udbg"] = nc.dram_tensor("udbg", [128, 4 * T], F32,
                                    kind="ExternalOutput").ap()
    prep = mode in ("A", 0, 1, 2)
    fin = mode != "A"
    if mode == "A" or mode == 3:
        io["x0"] = nc.dram_tensor("x0", [D, T], F32, kind="ExternalInput").ap()
    if mode == "A":
        io["win"] = nc.dram_tensor("win", [4, 128, 32 * 128], F32,
                                   kind="ExternalInput").ap()
        io["bin"] = nc.dram_tensor("bin", [128, 4], F32,
                                   kind="ExternalInput").ap()
    else:
        io["h_in"] = nc.dram_tensor("h_in", [128, 4 * T], F32,
                                    kind="ExternalInput").ap()
        io["w_in"] = nc.dram_tensor("w_in", [128, 8 * T], F32,
                                    kind="ExternalInput").ap()
        io["tb_in"] = nc.dram_tensor("tb_in", [128, 8 * T], F32,
                                     kind="ExternalInput").ap()
        io["st_in"] = nc.dram_tensor("st_in", [128, 8], F32,
                                     kind="ExternalInput").ap()
        io["plf"] = nc.dram_tensor("plf", [128, 36], F32,
                                   kind="ExternalInput").ap()
        io["w1"] = nc.dram_tensor("w1", [16, 128, 4 * 128], F32,
                                  kind="ExternalInput").ap()
        io["b1"] = nc.dram_tensor("b1", [128, 16], F32,
                                  kind="ExternalInput").ap()
        io["w2"] = nc.dram_tensor("w2", [4, 128, 16 * 128], F32,
                                  kind="ExternalInput").ap()
        io["b2"] = nc.dram_tensor("b2", [128, 4], F32,
                                  kind="ExternalInput").ap()
    if prep:
        io["plp"] = nc.dram_tensor("plp", [128, 36], F32,
                                   kind="ExternalInput").ap()
        io["w_out"] = nc.dram_tensor("w_out", [128, 8 * T], F32,
                                     kind="ExternalOutput").ap()
        io["tb_out"] = nc.dram_tensor("tb_out", [128, 8 * T], F32,
                                      kind="ExternalOutput").ap()
        io["st_out"] = nc.dram_tensor("st_out", [128, 16], F32,
                                      kind="ExternalOutput").ap()
    if mode != 3:
        io["h_out"] = nc.dram_tensor("h_out", [128, 4 * T], F32,
                                     kind="ExternalOutput").ap()
    else:
        io["lnl"] = nc.dram_tensor("lnl", [128, 8], F32,
                                   kind="ExternalInput").ap()
        io["wout"] = nc.dram_tensor("wout", [32, 128, 4 * 128], F32,
                                    kind="ExternalInput").ap()
        io["bout"] = nc.dram_tensor("bout", [128, 32], F32,
                                    kind="ExternalInput").ap()
        io["xout"] = nc.dram_tensor("xout", [D, T], F32,
                                    kind="ExternalOutput").ap()
    ctx = Ctx(nc)
    emit_l0a(ctx, io, mode)
    ctx.finish()
    return nc


def _p4(v):
    return np.ascontiguousarray(np.asarray(v).reshape(4, 128).T)


def stage_l0a_common(inp):
    win = tile_img_s(np.asarray(inp["L0_attn_Win"]).T, 4, 32, 128)
    wout = tile_img_s(np.asarray(inp["L0_attn_Wout"]).T, 32, 4, 128)
    pls = []
    w1s, b1s, w2s, b2s = [], [], [], []
    for lay in range(4):
        cols = [inp["L0_attn_ln1_w"][lay], inp["L0_attn_ln1_b"][lay],
                inp["L0_attn_ln2_w"][lay], inp["L0_attn_ln2_b"][lay],
                inp["L0_attn_sc_lin"][lay], inp["L0_attn_phz_re"][lay],
                inp["L0_attn_phz_im"][lay], inp["L0_attn_lc_re"][lay],
                inp["L0_attn_lc_im"][lay]]
        pls.append(np.ascontiguousarray(
            np.concatenate([_p4(c) for c in cols], axis=1)))
        w1s.append(tile_img_s(np.asarray(inp["L0_attn_W1"][lay]).T, 16, 4, 128))
        b1s.append(np.ascontiguousarray(
            np.asarray(inp["L0_attn_b1"][lay]).reshape(16, 128).T))
        w2s.append(tile_img_s(np.asarray(inp["L0_attn_W2"][lay]).T, 4, 16, 128))
        b2s.append(_p4(inp["L0_attn_b2"][lay]))
    lnl = np.ascontiguousarray(np.concatenate(
        [_p4(inp["L0_attn_lnl_w"]), _p4(inp["L0_attn_lnl_b"])], axis=1))
    return dict(win=win, bin=_p4(inp["L0_attn_bin"]), wout=wout,
                bout=np.ascontiguousarray(
                    np.asarray(inp["L0_attn_bout"]).reshape(32, 128).T),
                lnl=lnl, pls=pls, w1s=w1s, b1s=b1s, w2s=w2s, b2s=b2s)


def kernel(**inp):
    inp = {k: np.asarray(v) for k, v in inp.items()}
    xs = []
    for c in range(8):
        b, m = c // 2, c % 2
        xs.append(np.ascontiguousarray(
            inp["L0_attn_x"][b, m * 1024:(m + 1) * 1024, :].T))
    t0rows = [np.full((128, 1), (c % 2) * 1024 - 1, np.float32)
              for c in range(8)]
    cores = list(range(8))

    s0a = stage_l0a_common(inp)
    s0f = stage_l0f_common(inp)
    s1a = stage_l1a_common(inp)
    s1f = stage_l1f_common(inp)

    ncA = build_l0a("A")
    maps = [dict(t0row=t0rows[c], x0=xs[c], win=s0a["win"], bin=s0a["bin"],
                 plp=s0a["pls"][0]) for c in range(8)]
    r = run_bass_kernel_spmd(ncA, maps, core_ids=cores).results
    h_cur = [q["h_out"] for q in r]
    w_cur = [q["w_out"] for q in r]
    tb_cur = [q["tb_out"] for q in r]
    st_cur = [q["st_out"] for q in r]
    for lay in range(4):
        st_in = [np.ascontiguousarray(st_cur[c][:, 0:8]) if c % 2 == 0
                 else np.ascontiguousarray(st_cur[c - 1][:, 8:16])
                 for c in range(8)]
        mode = lay if lay < 3 else 3
        ncB = build_l0a(mode)
        maps = []
        for c in range(8):
            mio = dict(t0row=t0rows[c], h_in=h_cur[c], w_in=w_cur[c],
                       tb_in=tb_cur[c], st_in=st_in[c], plf=s0a["pls"][lay],
                       w1=s0a["w1s"][lay], b1=s0a["b1s"][lay],
                       w2=s0a["w2s"][lay], b2=s0a["b2s"][lay])
            if mode == 3:
                mio.update(x0=xs[c], lnl=s0a["lnl"], wout=s0a["wout"],
                           bout=s0a["bout"])
            else:
                mio.update(plp=s0a["pls"][lay + 1])
            maps.append(mio)
        r = run_bass_kernel_spmd(ncB, maps, core_ids=cores).results
        if mode != 3:
            h_cur = [q["h_out"] for q in r]
            w_cur = [q["w_out"] for q in r]
            tb_cur = [q["tb_out"] for q in r]
            st_cur = [q["st_out"] for q in r]
    x1 = [q["xout"] for q in r]

    nc = build_l0f()
    common = stage_l0f_common(inp)
    maps = [dict(common, xin=np.ascontiguousarray(x1[c])) for c in range(8)]
    r = run_bass_kernel_spmd(nc, maps, core_ids=cores).results
    x2 = [q["xout"] for q in r]

    nc1 = build_l1a_qkv()
    maps = [dict(xin=x2[c], wqk=s1a["wqk"], wv=s1a["wv"]) for c in range(8)]
    r1 = run_bass_kernel_spmd(nc1, maps, core_ids=cores).results
    kfull, vfull = {}, {}
    for b in range(4):
        kfull[b] = np.ascontiguousarray(
            np.concatenate([r1[2 * b]["kf"], r1[2 * b + 1]["kf"]], axis=1))
        vfull[b] = np.ascontiguousarray(
            np.concatenate([r1[2 * b]["vt"], r1[2 * b + 1]["vt"]], axis=0))
    x3 = [None] * 8
    for m, cs_ in ((0, [0, 2, 4, 6]), (1, [1, 3, 5, 7])):
        nc2 = build_l1a_att(m * 8)
        maps = [dict(xin=x2[c], qf=r1[c]["qf"], kfull=kfull[c // 2],
                     vfull=vfull[c // 2], wdT=s1a["wdT"], tri=s1a["tri"],
                     ident=s1a["ident"]) for c in cs_]
        r2 = run_bass_kernel_spmd(nc2, maps, core_ids=cs_).results
        for i, c in enumerate(cs_):
            x3[c] = r2[i]["xout"]

    nc = build_l1f()
    maps = [dict(s1f, xin=np.ascontiguousarray(x3[c])) for c in range(8)]
    r = run_bass_kernel_spmd(nc, maps, core_ids=cores).results

    out = np.empty((4, 2048, 4096), np.float32)
    for c in range(8):
        b, m = c // 2, c % 2
        out[b, m * 1024:(m + 1) * 1024, :] = r[c]["xout"].T
    return out
